# version 04 (final)
# baseline (speedup 1.0000x reference)
from contextlib import ExitStack

import numpy as np
import concourse.bass as bass
import concourse.mybir as mybir
from concourse.bass_utils import run_bass_kernel_spmd

F32 = mybir.dt.float32
BF16 = mybir.dt.bfloat16
I32 = mybir.dt.int32
AF = mybir.ActivationFunctionType
ALU = mybir.AluOpType

D = 2048
T = 4096
TM = 2048
NCH = D // 128
TT = 2048
NTT = T // TT
NSUB = TT // 512
EPS = 1e-5

PROJ_PAD = 4352
XBC = 2176
NH = 32
INTER = 8192


class Ctx:
    pass


def new_sem(ctx, name):
    h = ctx.es.enter_context(ctx.nc.semaphore(name))
    ctx.cnt[name] = 0
    ctx.sems[name] = h
    return h


def inc(ctx, instr, name, k=1):
    instr.then_inc(ctx.sems[name], k)
    ctx.cnt[name] += k
    return ctx.cnt[name]


def cnt(ctx, name):
    return ctx.cnt[name]


def wait(eng, ctx, name, val=None):
    eng.wait_ge(ctx.sems[name], ctx.cnt[name] if val is None else val)


def sbuf(ctx, name, shape, dt):
    es = getattr(ctx, "pes", None) or ctx.es
    return es.enter_context(ctx.nc.sbuf_tensor(name, shape, dt))


def psum(ctx, name, shape, dt=F32):
    es = getattr(ctx, "pes", None) or ctx.es
    return es.enter_context(ctx.nc.psum_tensor(name, shape, dt))


def begin_scope(ctx):
    ctx.pes = ExitStack()
    ctx.pes.__enter__()


def end_scope(ctx):
    ctx.pes.__exit__(None, None, None)
    ctx.pes = None


ENGS = ("sync", "gpsimd", "tensor", "vector", "scalar")


def barrier(ctx):
    for engname, semnames in ctx.dma_fence.items():
        sec = getattr(ctx.block, engname)

        @sec
        def _(eng, _sn=tuple(sorted(semnames))):
            for sn in _sn:
                wait(eng, ctx, sn)
    ctx.dma_fence = {}
    ctx.phase += 1
    tgt = 5 * ctx.phase
    for engname in ENGS:
        sec = getattr(ctx.block, engname)

        @sec
        def _(eng, _tgt=tgt):
            eng.sem_inc(ctx.sems["BAR"], 1)
            eng.wait_ge(ctx.sems["BAR"], _tgt)


def fence_dma(ctx, engname, semname):
    ctx.dma_fence.setdefault(engname, set()).add(semname)


def build_program(dbg=(), upto=None, start=None, feed=(), no_race_check=False):
    dbg = set(dbg)
    feed = set(feed)
    if no_race_check:
        nc = bass.Bass("TRN2", target_bir_lowering=False,
                       detect_race_conditions=False)
    else:
        nc = bass.Bass("TRN2", target_bir_lowering=False)
    ctx = Ctx()
    ctx.nc = nc
    ctx.es = ExitStack()
    ctx.es.__enter__()
    ctx.phase = 0
    ctx.cnt = {}
    ctx.sems = {}
    ctx.dbg = dbg
    ctx.dma_fence = {}

    def dram_in(name, shape, dt=F32):
        return nc.dram_tensor(name, shape, dt, kind="ExternalInput").ap()

    def dram_tmp(name, shape, dt):
        if name in feed:
            return nc.dram_tensor(name, shape, dt, kind="ExternalInput").ap()
        if name in dbg:
            return nc.dram_tensor(name, shape, dt, kind="ExternalOutput").ap()
        return nc.dram_tensor(name, shape, dt).ap()

    ctx.dram_in = dram_in
    ctx.dram_tmp = dram_tmp

    xT = dram_in("xT", [D, T])
    ln1_w = dram_in("ln1_w", [128, NCH])
    w_in = dram_in("w_in", [34, 16, 128, 128])
    w_ffn = dram_in("w_ffn", [16, 16, 128, 128])
    ffn_scale = dram_in("ffn_scale", [128, NCH])
    ones_f = dram_in("ones_f", [128, 128])
    conv_w = dram_in("conv_w", [XBC, 4])
    conv_b = dram_in("conv_b", [128, 17])
    a_log = dram_in("a_log", [NH, 1])
    dt_bias = dram_in("dt_bias", [NH, 1])
    d_rep = dram_in("d_rep", [128, NCH])
    mnorm_w = dram_in("mnorm_w", [128, NCH])
    ln2_w = dram_in("ln2_w", [128, NCH])
    w_out = dram_in("w_out", [16, 16, 128, 128])
    w_mlp_in = dram_in("w_mlp_in", [128, 16, 128, 128])
    w_mlp_out = dram_in("w_mlp_out", [16, 64, 128, 128])
    ident32 = dram_in("ident32", [NH, NH])
    scan_mask = dram_in("scan_mask", [128, 128])
    ident_f = dram_in("ident_f", [128, 128])
    ident8 = dram_in("ident8", [8, 8])
    side = dram_in("side", [1, 1], I32)
    wq_t = dram_in("wq_t", [16, 2, 16, 128, 64])
    wk_t = dram_in("wk_t", [16, 2, 16, 128, 64])
    wv_r = dram_in("wv_r", [D, D])
    wg_t = dram_in("wg_t", [16, 16, 128, 128])
    wo_t = dram_in("wo_t", [16, 16, 128, 128])
    ws1T = dram_in("ws1T", [D, 16])
    ws2T = dram_in("ws2T", [16, 1024])
    bs2c = dram_in("bs2c", [128, 8])
    bs2b = dram_in("bs2b", [128, 1024])
    gnw_in = dram_in("gnw", [128, 1])
    cos_k = dram_in("cos_k", [64, T])
    sin_k = dram_in("sin_k", [64, T])
    cos_q = dram_in("cos_q", [64, TM])
    sin_q = dram_in("sin_q", [64, TM])
    masksT = dram_in("masksT", [16, 256, 128])
    w_router = dram_in("w_router", [1, 16, 128, 8])
    w_gate = dram_in("w_gate", [8, 16, 16, 128, 128])
    w_up = dram_in("w_up", [8, 16, 16, 128, 128])
    w_down = dram_in("w_down", [8, 16, 16, 128, 128])
    w_sg = dram_in("w_sg", [16, 16, 128, 128])
    w_su = dram_in("w_su", [16, 16, 128, 128])
    w_sd = dram_in("w_sd", [16, 16, 128, 128])

    xn_d = dram_tmp("xn", [D, T], BF16)
    xr_d = dram_tmp("xr", [D, T], BF16)
    xz_d = dram_tmp("xz", [PROJ_PAD, T], BF16)
    ffn_d = dram_tmp("ffn0", [D, T], BF16)
    xconv_t = dram_tmp("xconv", [32, 17, 128, 128], BF16)
    y_d = dram_tmp("y_scan", [D, T], BF16)
    yn_d = dram_tmp("yn", [D, T], BF16)
    u_d = dram_tmp("u", [D, T], F32)
    un_d = dram_tmp("un", [D, T], BF16)
    swiglu_d = dram_tmp("swiglu", [INTER, T], BF16)
    x1_d = dram_tmp("x1", [D, T], F32)
    xn1_d = dram_tmp("xn1", [D, T], BF16)
    xn1m_d = dram_tmp("xn1m", [D, TM], BF16)
    x1m_d = dram_tmp("x1m", [D, TM], F32)
    k_d = dram_tmp("k_rope", [D, T], BF16)
    vtm_d = dram_tmp("v_tm", [T, D], BF16)
    q_d = dram_tmp("q_rope", [D, TM], BF16)
    g_d = dram_tmp("g_proj", [D, TM], BF16)
    es_d = dram_tmp("es", [1024, T], BF16)
    enz_d = dram_tmp("enz", [1024, T], F32)
    enzm_d = dram_tmp("enzm", [1024, TM], F32)
    estm_d = dram_tmp("es_tm", [T, 1024], BF16)
    og_d = dram_tmp("og", [D, TM], BF16)
    moe_d = dram_tmp("moe", [D, TM], F32)
    x2m = nc.dram_tensor("x2m", [D, TM], F32, kind="ExternalOutput").ap()

    ctx.block = ctx.es.enter_context(nc.Block())
    for s in ("BAR", "ld0", "ld", "wl", "mm", "ev", "sc", "st", "wf", "ac",
              "sa", "ex", "bc", "my", "ms", "eg", "ew", "ey",
              "wla", "wlb", "aca", "acb", "lda", "ldb",
              "gld", "gst", "xca", "xcb", "gla", "glb"):
        new_sem(ctx, s)

    ones_t = sbuf(ctx, "ones_t", [128, 128], BF16)
    ctx.ones = ones_t

    @ctx.block.gpsimd
    def _(g):
        ii = g.dma_start(out=ones_t[:, :], in_=ones_f[:, :])
        inc(ctx, ii, "ld0", 16)
        wait(g, ctx, "ld0")

    barrier(ctx)

    phases = [
        ("A", lambda: phase_ln_rms(ctx, src=xT, out_ln=xn_d, out_rms=xr_d,
                                   w_rms=ln1_w, tag="A")),
        ("B", lambda: phase_matmul(ctx, tag="B", act=xr_d, KT=16,
                                   w_tiled=w_in, MT=34, out=xz_d)),
        ("C", lambda: phase_ffn(ctx, tag="C", act=xn_d, w_tiled=w_ffn,
                                scale=ffn_scale, out=ffn_d)),
        ("D", lambda: phase_conv(ctx, xz_d, conv_w, conv_b, xconv_t)),
        ("E", lambda: phase_scan(ctx, xz_d, xconv_t, a_log, dt_bias, d_rep,
                                 ident32, scan_mask, ident_f, y_d)),
        ("F", lambda: phase_gate_mnorm(ctx, y_d, xz_d, mnorm_w, yn_d)),
        ("F2", lambda: phase_matmul(ctx, tag="F2", act=yn_d, KT=16,
                                    w_tiled=w_out, MT=16, out=u_d, out_dt=F32,
                                    res=((xn_d, False),))),
        ("G", lambda: phase_rms(ctx, src=u_d, src_f32=True, w_rms=ln2_w,
                                out=un_d)),
        ("H", lambda: phase_mlp_in(ctx, un_d, w_mlp_in, swiglu_d)),
        ("I", lambda: phase_matmul(ctx, tag="I", act=swiglu_d, KT=64,
                                   w_tiled=w_mlp_out, MT=16, out=x1_d,
                                   out_dt=F32, TTo=1024,
                                   res=((u_d, True), (xT, True),
                                        (ffn_d, False)))),
        ("J", lambda: phase_ln_rms(ctx, src=x1_d, out_ln=xn1_d, out_rms=None,
                                   w_rms=ln1_w, tag="J", do_rms=False)),
        ("L", lambda: phase_proj_rope(ctx, "L", xn1_d, wk_t, cos_k, sin_k,
                                      k_d, t_total=T)),
        ("L2", lambda: phase_matmulA(ctx, "L2", xn1_d, wv_r, vtm_d, t_total=T)),
        ("N", lambda: phase_spath(ctx, xn1_d, ws1T, ws2T, bs2c, bs2b, es_d,
                                  enz_d, estm_d, tag="N")),
        ("K", lambda: phase_gather(ctx, side, xn1_d, x1_d, enz_d, xn1m_d,
                                   x1m_d, enzm_d, tag="K")),
        ("M", lambda: phase_proj_rope(ctx, "M", xn1m_d, wq_t, cos_q, sin_q,
                                      q_d, t_total=TM)),
        ("M2", lambda: phase_matmul(ctx, tag="M2", act=xn1m_d, KT=16,
                                    w_tiled=wg_t, MT=16, out=g_d,
                                    t_total=TM)),
        ("O", lambda: phase_attn(ctx, q_d, g_d, k_d, vtm_d, es_d, enzm_d,
                                 estm_d, masksT, gnw_in, ident_f, og_d,
                                 tag="O")),
        ("Q", lambda: phase_moe(ctx, xn1m_d, w_router, w_gate, w_up, w_down,
                                w_sg, w_su, w_sd, ident8, ident_f, moe_d,
                                tag="Q")),
        ("P", lambda: phase_matmul(ctx, tag="P", act=og_d, KT=16,
                                   w_tiled=wo_t, MT=16, out=x2m, out_dt=F32,
                                   t_total=TM,
                                   res=((x1m_d, True), (moe_d, True)))),
    ]
    started = start is None
    for pname, pfn in phases:
        if not started:
            if pname == start:
                started = True
            else:
                continue
        pfn()
        if upto == pname:
            return fin(ctx)
    return fin(ctx)


def fin(ctx):
    barrier(ctx)
    ctx.es.__exit__(None, None, None)
    return ctx.nc


def phase_ln_rms(ctx, src, out_ln, out_rms, w_rms, tag, do_rms=True):
    begin_scope(ctx)
    S = lambda n: n
    sm = lambda n: ctx.sems[n]

    act = sbuf(ctx, tag + "act", [128, NCH * TT], BF16)
    tmp = sbuf(ctx, tag + "tmp", [128, TT], F32)
    sq = sbuf(ctx, tag + "sq", [128, 2 * TT], BF16)
    mb = sbuf(ctx, tag + "mb", [128, TT], F32)
    m2 = sbuf(ctx, tag + "m2", [128, TT], F32)
    var = sbuf(ctx, tag + "var", [128, TT], F32)
    sd = sbuf(ctx, tag + "sd", [128, TT], F32)
    rr = sbuf(ctx, tag + "rr", [128, TT], F32)
    mrr = sbuf(ctx, tag + "mrr", [128, TT], F32)
    r2 = sbuf(ctx, tag + "r2", [128, TT], F32)
    xout = sbuf(ctx, tag + "xo", [128, 2 * TT], BF16)
    wt = sbuf(ctx, tag + "wt", [128, NCH], F32)
    ps = psum(ctx, tag + "ps", [128, 8, 512])
    ones = ctx.ones

    @ctx.block.sync
    def _(s):
        ii = s.dma_start(out=wt[:, :], in_=w_rms[:, :])
        inc(ctx, ii, S("ld"), 16)

    for tt in range(NTT):
        t0 = tt * TT

        @ctx.block.gpsimd
        def _(g, tt=tt):
            if tt > 0:
                wait(g, ctx, S("st"))
            for c in range(NCH):
                ii = g.dma_start(out=act[:, c * TT:(c + 1) * TT],
                                 in_=src[c * 128:(c + 1) * 128, t0:t0 + TT])
                inc(ctx, ii, S("gld"), 16)

        ld_all = cnt(ctx, S("gld"))
        mm_b1 = cnt(ctx, S("mm"))
        sc_b1 = cnt(ctx, S("sc"))

        @ctx.block.scalar
        def _(sce):
            sce.wait_ge(sm("gld"), ld_all)
            for c in range(NCH):
                if c >= 2:
                    sce.wait_ge(sm("mm"), mm_b1 + (c - 1))
                half = (c % 2) * TT
                ii = sce.activation(out=sq[:, half:half + TT],
                                    in_=act[:, c * TT:(c + 1) * TT], func=AF.Square)
                inc(ctx, ii, S("sc"), 1)

        @ctx.block.tensor
        def _(t):
            wait(t, ctx, S("ev"))
            t.wait_ge(sm("gld"), ld_all)
            for c in range(NCH):
                for sub in range(NSUB):
                    t.matmul(ps[:, sub, :], ones[:, :],
                             act[:, c * TT + sub * 512:c * TT + (sub + 1) * 512],
                             start=(c == 0), stop=(c == NCH - 1))
                t.wait_ge(sm("sc"), sc_b1 + c + 1)
                half = (c % 2) * TT
                for sub in range(NSUB):
                    lm = t.matmul(ps[:, 4 + sub, :], ones[:, :],
                                  sq[:, half + sub * 512:half + (sub + 1) * 512],
                                  start=(c == 0), stop=(c == NCH - 1))
                inc(ctx, lm, S("mm"), 1)

        @ctx.block.scalar
        def _(sce):
            wait(sce, ctx, S("mm"))
            for sub in range(NSUB):
                sce.activation(out=mb[:, sub * 512:(sub + 1) * 512], in_=ps[:, sub, :],
                               func=AF.Copy, scale=1.0 / D)
            ii = sce.activation(out=m2[:, :], in_=mb[:, :], func=AF.Square)
            inc(ctx, ii, S("sc"), 1)

        @ctx.block.vector
        def _(v):
            wait(v, ctx, S("sc"))
            for sub in range(NSUB):
                v.scalar_tensor_tensor(
                    out=var[:, sub * 512:(sub + 1) * 512],
                    in0=ps[:, 4 + sub, :], scalar=1.0 / D,
                    in1=m2[:, sub * 512:(sub + 1) * 512],
                    op0=ALU.mult, op1=ALU.subtract)
            lm = v.tensor_scalar_add(var[:, :], var[:, :], EPS)
            inc(ctx, lm, S("ev"), 1)

        @ctx.block.scalar
        def _(sce):
            wait(sce, ctx, S("ev"))
            ii = sce.activation(out=sd[:, :], in_=var[:, :], func=AF.Sqrt)
            inc(ctx, ii, S("sc"), 1)

        @ctx.block.vector
        def _(v):
            wait(v, ctx, S("sc"))
            v.reciprocal(rr[:, :], sd[:, :])
            ii = v.tensor_mul(mrr[:, :], mb[:, :], rr[:, :])
            inc(ctx, ii, S("ev"), 1)

        ev_xn = cnt(ctx, S("ev"))

        @ctx.block.vector
        def _(v):
            wait(v, ctx, S("ev"), ev_xn)
            for c in range(NCH):
                v.tensor_mul(tmp[:, :], act[:, c * TT:(c + 1) * TT], rr[:, :])
                ii = v.tensor_sub(act[:, c * TT:(c + 1) * TT], tmp[:, :], mrr[:, :])
                inc(ctx, ii, S("ev"), 1)

        @ctx.block.sync
        def _(s):
            for c in range(NCH):
                s.wait_ge(sm("ev"), ev_xn + c + 1)
                ii = s.dma_start(out=out_ln[c * 128:(c + 1) * 128, t0:t0 + TT],
                                 in_=act[:, c * TT:(c + 1) * TT])
                inc(ctx, ii, S("st"), 16)

        if not do_rms:
            continue
        mm_b2 = cnt(ctx, S("mm"))
        sc_b2 = cnt(ctx, S("sc"))

        @ctx.block.scalar
        def _(sce):
            for c in range(NCH):
                sce.wait_ge(sm("ev"), ev_xn + c + 1)
                if c >= 2:
                    sce.wait_ge(sm("mm"), mm_b2 + (c - 1))
                half = (c % 2) * TT
                ii = sce.activation(out=sq[:, half:half + TT],
                                    in_=act[:, c * TT:(c + 1) * TT], func=AF.Square)
                inc(ctx, ii, S("sc"), 1)

        @ctx.block.tensor
        def _(t):
            wait(t, ctx, S("ev"))
            for c in range(NCH):
                t.wait_ge(sm("sc"), sc_b2 + c + 1)
                half = (c % 2) * TT
                for sub in range(NSUB):
                    lm = t.matmul(ps[:, sub, :], ones[:, :],
                                  sq[:, half + sub * 512:half + (sub + 1) * 512],
                                  start=(c == 0), stop=(c == NCH - 1))
                inc(ctx, lm, S("mm"), 1)

        @ctx.block.vector
        def _(v):
            wait(v, ctx, S("mm"))
            for sub in range(NSUB):
                lm = v.tensor_scalar(
                    out=var[:, sub * 512:(sub + 1) * 512], in0=ps[:, sub, :],
                    scalar1=1.0 / D, scalar2=EPS, op0=ALU.mult, op1=ALU.add)
            inc(ctx, lm, S("ev"), 1)

        @ctx.block.scalar
        def _(sce):
            wait(sce, ctx, S("ev"))
            ii = sce.activation(out=sd[:, :], in_=var[:, :], func=AF.Sqrt)
            inc(ctx, ii, S("sc"), 1)

        @ctx.block.vector
        def _(v):
            wait(v, ctx, S("sc"))
            ii = v.reciprocal(r2[:, :], sd[:, :])
            inc(ctx, ii, S("ev"), 1)

        sc_b3 = cnt(ctx, S("sc"))
        ev_b3 = cnt(ctx, S("ev"))
        st_b3 = cnt(ctx, S("st"))

        @ctx.block.vector
        def _(v):
            wait(v, ctx, S("ev"), ev_b3)
            for c in range(NCH):
                if c >= 1:
                    v.wait_ge(sm("sc"), sc_b3 + c)
                ii = v.tensor_mul(tmp[:, :], act[:, c * TT:(c + 1) * TT], r2[:, :])
                inc(ctx, ii, S("ev"), 1)

        @ctx.block.scalar
        def _(sce):
            wait(sce, ctx, S("ld"))
            for c in range(NCH):
                sce.wait_ge(sm("ev"), ev_b3 + c + 1)
                if c >= 2:
                    sce.wait_ge(sm("st"), st_b3 + 16 * c)
                half = (c % 2) * TT
                ii = sce.activation(out=xout[:, half:half + TT], in_=tmp[:, :],
                                    func=AF.Copy, scale=wt[:, c:c + 1])
                inc(ctx, ii, S("sc"), 1)

        @ctx.block.sync
        def _(s):
            for c in range(NCH):
                s.wait_ge(sm("sc"), sc_b3 + c + 1)
                half = (c % 2) * TT
                ii = s.dma_start(out=out_rms[c * 128:(c + 1) * 128, t0:t0 + TT],
                                 in_=xout[:, half:half + TT])
                inc(ctx, ii, S("st"), 16)

    fence_dma(ctx, "sync", S("st"))
    barrier(ctx)
    end_scope(ctx)


def phase_matmul(ctx, tag, act, KT, w_tiled, MT, out, out_dt=BF16,
                TTo=TT, res=(), act_f32=False, t_total=T):
    begin_scope(ctx)
    S = lambda n: n
    sm = lambda n: ctx.sems[n]
    NSUBo = TTo // 512
    ntt = t_total // TTo

    a_sb = sbuf(ctx, tag + "a", [128, KT * TTo], BF16)
    w_sb = sbuf(ctx, tag + "w", [128, 2 * KT * 128], BF16)
    o_sb = sbuf(ctx, tag + "o", [128, 2 * TTo], out_dt)
    r_sb = [sbuf(ctx, tag + "r%d" % i, [128, 2 * TTo], F32 if rf else BF16)
            for i, (rd, rf) in enumerate(res)]
    rtmp = sbuf(ctx, tag + "rt", [128, TTo], F32) if res else None
    ps = psum(ctx, tag + "ps", [128, 2 * NSUBo, 512])

    b_wl2 = (cnt(ctx, S("wla")), cnt(ctx, S("wlb")))
    b_mm = cnt(ctx, S("mm"))
    b_ev = cnt(ctx, S("ev"))
    b_st = cnt(ctx, S("st"))
    b_rl2 = (cnt(ctx, S("aca")), cnt(ctx, S("acb")))
    nres = len(res)
    for tt in range(ntt):
        t0 = tt * TTo

        @ctx.block.sync
        def _(s, tt=tt, t0=t0):
            if tt > 0:
                wait(s, ctx, S("mm"))
            if not act_f32:
                for c in range(KT):
                    ii = s.dma_start(out=a_sb[:, c * TTo:(c + 1) * TTo],
                                     in_=act[c * 128:(c + 1) * 128, t0:t0 + TTo])
                    inc(ctx, ii, S("ld"), 16)

        if act_f32:
            @ctx.block.gpsimd
            def _(g, tt=tt, t0=t0):
                if tt > 0:
                    wait(g, ctx, S("mm"))
                for c in range(KT):
                    ii = g.dma_start(out=a_sb[:, c * TTo:(c + 1) * TTo],
                                     in_=act[c * 128:(c + 1) * 128, t0:t0 + TTo])
                    inc(ctx, ii, S("gld"), 16)

        ldsem = "gld" if act_f32 else "ld"
        ld_hi = cnt(ctx, S(ldsem))

        @ctx.block.gpsimd
        def _(g, tt=tt, t0=t0):
            for m in range(MT):
                gi = tt * MT + m
                if gi >= 2:
                    g.wait_ge(sm("mm"), b_mm + gi - 1)
                buf = gi % 2
                ii = g.dma_start(
                    out=w_sb[:, buf * KT * 128:(buf + 1) * KT * 128].rearrange(
                        "p (k q) -> p k q", q=128),
                    in_=w_tiled[m].rearrange("k p q -> p k q"))
                inc(ctx, ii, S("wla" if buf == 0 else "wlb"), 16)
                if nres:
                    if gi >= 2:
                        g.wait_ge(sm("ev"), b_ev + gi - 1)
                    for i, (rd, rf) in enumerate(res):
                        ii = g.dma_start(
                            out=r_sb[i][:, (gi % 2) * TTo:(gi % 2 + 1) * TTo],
                            in_=rd[m * 128:(m + 1) * 128, t0:t0 + TTo])
                        inc(ctx, ii, S("aca" if gi % 2 == 0 else "acb"), 16)

        @ctx.block.tensor
        def _(t, tt=tt, ld_hi=ld_hi, ldsem=ldsem):
            t.wait_ge(sm(ldsem), ld_hi)
            for m in range(MT):
                gi = tt * MT + m
                t.wait_ge(sm("wla" if gi % 2 == 0 else "wlb"),
                          b_wl2[gi % 2] + 16 * (gi // 2 + 1))
                buf = gi % 2
                bs = (gi % 2) * NSUBo
                if gi >= 2:
                    t.wait_ge(sm("ev"), b_ev + gi - 1)
                for k in range(KT):
                    for sub in range(NSUBo):
                        lm = t.matmul(
                            ps[:, bs + sub, :],
                            w_sb[:, buf * KT * 128 + k * 128:buf * KT * 128 + (k + 1) * 128],
                            a_sb[:, k * TTo + sub * 512:k * TTo + (sub + 1) * 512],
                            start=(k == 0), stop=(k == KT - 1))
                inc(ctx, lm, S("mm"), 1)

        @ctx.block.vector
        def _(v, tt=tt):
            for m in range(MT):
                gi = tt * MT + m
                v.wait_ge(sm("mm"), b_mm + gi + 1)
                bs = (gi % 2) * NSUBo
                ob = (gi % 2) * TTo
                if gi >= 2:
                    v.wait_ge(sm("st"), b_st + 16 * gi)
                if nres:
                    v.wait_ge(sm("aca" if gi % 2 == 0 else "acb"),
                              b_rl2[gi % 2] + 16 * nres * (gi // 2 + 1))
                    for sub in range(NSUBo):
                        sl = slice(sub * 512, (sub + 1) * 512)
                        v.tensor_add(rtmp[:, sl], ps[:, bs + sub, :],
                                     r_sb[0][:, ob + sub * 512:ob + (sub + 1) * 512])
                    for i in range(1, nres):
                        v.tensor_add(rtmp[:, :], rtmp[:, :],
                                     r_sb[i][:, ob:ob + TTo])
                    lm = v.tensor_copy(o_sb[:, ob:ob + TTo], rtmp[:, :])
                else:
                    for sub in range(NSUBo):
                        lm = v.tensor_copy(o_sb[:, ob + sub * 512:ob + (sub + 1) * 512],
                                           ps[:, bs + sub, :])
                inc(ctx, lm, S("ev"), 1)

        @ctx.block.sync
        def _(s, tt=tt, t0=t0):
            for m in range(MT):
                gi = tt * MT + m
                s.wait_ge(sm("ev"), b_ev + gi + 1)
                ob = (gi % 2) * TTo
                ii = s.dma_start(out=out[m * 128:(m + 1) * 128, t0:t0 + TTo],
                                 in_=o_sb[:, ob:ob + TTo])
                inc(ctx, ii, S("st"), 16)

    fence_dma(ctx, "sync", S("st"))
    barrier(ctx)
    end_scope(ctx)


def phase_ffn(ctx, tag, act, w_tiled, scale, out):
    begin_scope(ctx)
    S = lambda n: n
    sm = lambda n: ctx.sems[n]
    KT = MT = 16
    FEPS = 1e-7

    a_sb = sbuf(ctx, tag + "a", [128, KT * TT], BF16)
    w_sb = sbuf(ctx, tag + "w", [128, 2 * KT * 128], BF16)
    f_sb = sbuf(ctx, tag + "f", [128, NCH * TT], BF16)
    sq = sbuf(ctx, tag + "sq", [128, 2 * TT], BF16)
    sums = sbuf(ctx, tag + "sm", [128, TT], F32)
    sumb = sbuf(ctx, tag + "smb", [128, TT], BF16)
    vn = sbuf(ctx, tag + "vn", [128, TT], F32)
    fac = sbuf(ctx, tag + "fac", [128, TT], F32)
    t1 = sbuf(ctx, tag + "t1", [128, TT], F32)
    t2 = sbuf(ctx, tag + "t2", [128, TT], F32)
    o_sb = sbuf(ctx, tag + "o", [128, 2 * TT], BF16)
    sc_t = sbuf(ctx, tag + "sct", [128, NCH], F32)
    ps = psum(ctx, tag + "ps", [128, 8, 512])
    ones = ctx.ones

    @ctx.block.sync
    def _(s):
        ii = s.dma_start(out=sc_t[:, :], in_=scale[:, :])
        inc(ctx, ii, S("ld"), 16)

    for tt in range(NTT):
        t0 = tt * TT

        @ctx.block.sync
        def _(s, tt=tt):
            if tt > 0:
                wait(s, ctx, S("mm"))
            for c in range(KT):
                ii = s.dma_start(out=a_sb[:, c * TT:(c + 1) * TT],
                                 in_=act[c * 128:(c + 1) * 128, t0:t0 + TT])
                inc(ctx, ii, S("ld"), 16)

        ld_hi = cnt(ctx, S("ld"))
        mm_b = cnt(ctx, S("mm"))
        sc_b = cnt(ctx, S("sc"))
        ev_b = cnt(ctx, S("ev"))
        ac_b = cnt(ctx, S("ac"))

        wl_b2 = (cnt(ctx, S("wla")), cnt(ctx, S("wlb")))

        @ctx.block.gpsimd
        def _(g, tt=tt, mm_b=mm_b):
            wait(g, ctx, S("mm"))
            for m in range(MT):
                if m >= 2:
                    g.wait_ge(sm("mm"), mm_b + m - 1)
                buf = (tt * MT + m) % 2
                ii = g.dma_start(
                    out=w_sb[:, buf * KT * 128:(buf + 1) * KT * 128].rearrange(
                        "p (k q) -> p k q", q=128),
                    in_=w_tiled[m].rearrange("k p q -> p k q"))
                inc(ctx, ii, S("wla" if buf == 0 else "wlb"), 16)

        @ctx.block.tensor
        def _(t, tt=tt, ld_hi=ld_hi, mm_b=mm_b, wl_b2=wl_b2):
            t.wait_ge(sm("ld"), ld_hi)
            wait(t, ctx, S("ev"))
            for m in range(MT):
                gi = tt * MT + m
                buf_ = (tt * MT + m) % 2
                t.wait_ge(sm("wla" if buf_ == 0 else "wlb"),
                          wl_b2[buf_] + 16 * (m // 2 + 1))
                buf = gi % 2
                bs = (m % 2) * 4
                if m >= 2:
                    t.wait_ge(sm("ev"), ev_b + (m - 1))
                for k in range(KT):
                    for sub in range(NSUB):
                        lm = t.matmul(
                            ps[:, bs + sub, :],
                            w_sb[:, buf * KT * 128 + k * 128:buf * KT * 128 + (k + 1) * 128],
                            a_sb[:, k * TT + sub * 512:k * TT + (sub + 1) * 512],
                            start=(k == 0), stop=(k == KT - 1))
                inc(ctx, lm, S("mm"), 1)

        @ctx.block.scalar
        def _(sce, mm_b=mm_b, ac_b=ac_b):
            wait(sce, ctx, S("ld"))
            for m in range(MT):
                sce.wait_ge(sm("mm"), mm_b + m + 1)
                bs = (m % 2) * 4
                for sub in range(NSUB):
                    lm = sce.activation(
                        out=f_sb[:, m * TT + sub * 512:m * TT + (sub + 1) * 512],
                        in_=ps[:, bs + sub, :], func=AF.Copy,
                        scale=sc_t[:, m:m + 1])
                inc(ctx, lm, S("ev"), 1)
                if m >= 2:
                    sce.wait_ge(sm("ac"), ac_b + m - 1)
                half = (m % 2) * TT
                ii = sce.activation(out=sq[:, half:half + TT],
                                    in_=f_sb[:, m * TT:(m + 1) * TT], func=AF.Square)
                inc(ctx, ii, S("sc"), 1)

        @ctx.block.vector
        def _(v, sc_b=sc_b, ac_b=ac_b):
            v.memset(sums[:, :], 0.0)
            for m in range(MT):
                v.wait_ge(sm("sc"), sc_b + m + 1)
                half = (m % 2) * TT
                ii = v.tensor_add(sums[:, :], sums[:, :], sq[:, half:half + TT])
                inc(ctx, ii, S("ac"), 1)

        @ctx.block.scalar
        def _(sce):
            wait(sce, ctx, S("ac"))
            ii = sce.activation(out=sumb[:, :], in_=sums[:, :], func=AF.Copy)
            inc(ctx, ii, S("sc"), 1)

        @ctx.block.tensor
        def _(t):
            wait(t, ctx, S("sc"))
            wait(t, ctx, S("ev"))
            for sub in range(NSUB):
                lm = t.matmul(ps[:, sub, :], ones[:, :],
                              sumb[:, sub * 512:(sub + 1) * 512],
                              start=True, stop=True)
            inc(ctx, lm, S("mm"), 1)

        @ctx.block.scalar
        def _(sce):
            wait(sce, ctx, S("mm"))
            for sub in range(NSUB):
                lm = sce.activation(out=vn[:, sub * 512:(sub + 1) * 512],
                                    in_=ps[:, sub, :], func=AF.Sqrt)
            inc(ctx, lm, S("sc"), 1)

        @ctx.block.vector
        def _(v):
            wait(v, ctx, S("sc"))
            ii = v.tensor_scalar_max(vn[:, :], vn[:, :], FEPS)
            inc(ctx, ii, S("ev"), 1)

        @ctx.block.scalar
        def _(sce):
            wait(sce, ctx, S("ev"))
            ii = sce.activation(out=t1[:, :], in_=vn[:, :], func=AF.Tanh)
            inc(ctx, ii, S("sc"), 1)

        @ctx.block.vector
        def _(v):
            wait(v, ctx, S("sc"))
            v.tensor_scalar_add(t2[:, :], vn[:, :], FEPS)
            v.reciprocal(t2[:, :], t2[:, :])
            v.tensor_mul(fac[:, :], t1[:, :], t2[:, :])
            v.tensor_mul(t1[:, :], vn[:, :], fac[:, :])
            v.tensor_scalar_max(t1[:, :], t1[:, :], FEPS)
            v.reciprocal(t2[:, :], t1[:, :])
            v.tensor_scalar_mul(t2[:, :], t2[:, :], 0.99)
            v.tensor_scalar_min(t2[:, :], t2[:, :], 1.0)
            ii = v.tensor_mul(fac[:, :], fac[:, :], t2[:, :])
            inc(ctx, ii, S("ev"), 1)

        ev_o = cnt(ctx, S("ev"))
        st_o = cnt(ctx, S("st"))

        @ctx.block.vector
        def _(v):
            wait(v, ctx, S("ev"), ev_o)
            for c in range(NCH):
                if c >= 2:
                    v.wait_ge(sm("st"), st_o + 16 * c)
                half = (c % 2) * TT
                ii = v.tensor_mul(o_sb[:, half:half + TT],
                                  f_sb[:, c * TT:(c + 1) * TT], fac[:, :])
                inc(ctx, ii, S("ev"), 1)

        @ctx.block.sync
        def _(s, t0=t0):
            for c in range(NCH):
                s.wait_ge(sm("ev"), ev_o + c + 1)
                half = (c % 2) * TT
                ii = s.dma_start(out=out[c * 128:(c + 1) * 128, t0:t0 + TT],
                                 in_=o_sb[:, half:half + TT])
                inc(ctx, ii, S("st"), 16)

    fence_dma(ctx, "sync", S("st"))
    barrier(ctx)
    end_scope(ctx)


def phase_conv(ctx, xz_d, conv_w, conv_b, xconv_t, tag="D"):
    begin_scope(ctx)
    S = lambda n: n
    sm = lambda n: ctx.sems[n]
    NR = 17

    cin = sbuf(ctx, tag + "in", [128, 2 * (T + 4)], BF16)
    cout = sbuf(ctx, tag + "out", [128, 2 * T], BF16)
    acc = sbuf(ctx, tag + "acc", [128, T], F32)
    cw = sbuf(ctx, tag + "cw", [128, NR * 4], F32)
    cb = sbuf(ctx, tag + "cb", [128, NR], F32)

    @ctx.block.sync
    def _(s):
        ii = s.dma_start(out=cw[:, :].rearrange("p (r k) -> p r k", k=4),
                         in_=conv_w.rearrange("(r p) k -> p r k", p=128))
        inc(ctx, ii, S("ld"), 16)
        ii = s.dma_start(out=cb[:, :], in_=conv_b.rearrange("(r p) o -> p (r o)", p=128))
        inc(ctx, ii, S("ld"), 16)

    b_ld = cnt(ctx, S("ld"))
    b_ev = cnt(ctx, S("ev"))
    b_sc = cnt(ctx, S("sc"))
    b_st = cnt(ctx, S("st"))

    @ctx.block.vector
    def _(v):
        v.memset(cin[:, 0:3], 0.0)
        v.memset(cin[:, T + 4:T + 7], 0.0)
        ii = v.engine_nop()
        inc(ctx, ii, S("ev"), 1)

    b_lp = (cnt(ctx, S("lda")), cnt(ctx, S("ldb")))
    b_st = cnt(ctx, S("gst"))

    @ctx.block.sync
    def _(s):
        wait(s, ctx, S("ev"))
        for r in range(NR):
            if r >= 2:
                s.wait_ge(sm("sc"), b_sc + r - 1)
            half = (r % 2) * (T + 4)
            ii = s.dma_start(out=cin[:, half + 3:half + 3 + T],
                             in_=xz_d[D + r * 128:D + (r + 1) * 128, :])
            inc(ctx, ii, S("lda" if r % 2 == 0 else "ldb"), 16)

    @ctx.block.vector
    def _(v):
        v.wait_ge(sm("ld"), b_ld)
        for r in range(NR):
            v.wait_ge(sm("lda" if r % 2 == 0 else "ldb"),
                      b_lp[r % 2] + 16 * (r // 2 + 1))
            if r >= 1:
                v.wait_ge(sm("sc"), b_sc + r)
            if r >= 2:
                v.wait_ge(sm("gst"), b_st + 16 * 4 * r)
            half = (r % 2) * (T + 4)
            v.tensor_scalar(out=acc[:, :], in0=cin[:, half:half + T],
                            scalar1=cw[:, 4 * r:4 * r + 1], scalar2=None,
                            op0=ALU.mult)
            for kk in range(1, 4):
                v.scalar_tensor_tensor(
                    out=acc[:, :], in0=cin[:, half + kk:half + kk + T],
                    scalar=cw[:, 4 * r + kk:4 * r + kk + 1], in1=acc[:, :],
                    op0=ALU.mult, op1=ALU.add)
            ii = v.engine_nop()
            inc(ctx, ii, S("ev"), 1)

    @ctx.block.scalar
    def _(sce):
        sce.wait_ge(sm("ld"), b_ld)
        for r in range(NR):
            sce.wait_ge(sm("ev"), b_ev + 1 + r + 1)
            oh = (r % 2) * T
            ii = sce.activation(out=cout[:, oh:oh + T], in_=acc[:, :],
                                func=AF.Silu, bias=cb[:, r:r + 1])
            inc(ctx, ii, S("sc"), 1)

    @ctx.block.gpsimd
    def _(g):
        for r in range(NR):
            g.wait_ge(sm("sc"), b_sc + r + 1)
            oh = (r % 2) * T
            for grp in range(4):
                ii = g.dma_start(
                    out=xconv_t[grp * 8:(grp + 1) * 8, r].rearrange(
                        "c p q -> p c q"),
                    in_=cout[:, oh + grp * 1024:oh + (grp + 1) * 1024].rearrange(
                        "p (c q) -> p c q", q=128))
                inc(ctx, ii, S("gst"), 16)

    fence_dma(ctx, "gpsimd", S("gst"))
    barrier(ctx)
    end_scope(ctx)


def phase_scan(ctx, xz_d, xconv_t, a_log, dt_bias, d_rep, ident32, scan_mask,
               ident_f, y_d, tag="E"):
    begin_scope(ctx)
    sm = lambda n: ctx.sems[n]
    S = lambda n: n
    NCK = 32
    NHD = 32

    dtp = sbuf(ctx, tag + "dtp", [128, T], BF16)
    d0 = sbuf(ctx, tag + "d0", [32, T], F32)
    dt_ = sbuf(ctx, tag + "dt", [32, T], F32)
    tmp = sbuf(ctx, tag + "tm", [32, T], F32)
    scum = sbuf(ctx, tag + "scum", [32, T], F32)
    qq = sbuf(ctx, tag + "q", [32, T], F32)
    zeros32 = sbuf(ctx, tag + "z32", [32, 128], F32)
    abias = sbuf(ctx, tag + "ab", [32, 1], F32)
    eab = sbuf(ctx, tag + "eab", [32, 1], F32)
    id32 = sbuf(ctx, tag + "id32", [32, 32], F32)
    ones32f = sbuf(ctx, tag + "o32f", [32, 128], F32)
    drep = sbuf(ctx, tag + "drep", [128, 16], F32)
    maskt = sbuf(ctx, tag + "msk", [128, 128], F32)
    idbf = sbuf(ctx, tag + "idbf", [128, 128], BF16)

    xcv = sbuf(ctx, tag + "xcv", [128, 2 * 17 * 128], BF16)
    xct = sbuf(ctx, tag + "xct", [128, 2 * 17 * 128], BF16)
    qt = sbuf(ctx, tag + "qt", [128, 32], F32)
    ptm = sbuf(ctx, tag + "ptm", [128, 128], BF16)
    brow = sbuf(ctx, tag + "br", [32, 2 * NHD * 128], F32)
    mt_t = sbuf(ctx, tag + "mt", [128, 2 * 128], BF16)
    mcl = sbuf(ctx, tag + "mcl", [128, 2 * 128], F32)
    gt = sbuf(ctx, tag + "gt", [128, 2 * 128], BF16)
    ebc = sbuf(ctx, tag + "ebc", [64, 2 * 128], F32)
    cce = sbuf(ctx, tag + "cce", [64, 2 * 128], BF16)
    wb = sbuf(ctx, tag + "wb", [128, 2 * 64], BF16)
    wcol = sbuf(ctx, tag + "wcol", [128, 2], F32)
    cc0 = sbuf(ctx, tag + "cc0", [64, 128], BF16)
    st_ = [sbuf(ctx, tag + "stA", [64, NHD * 64], BF16),
           sbuf(ctx, tag + "stB", [64, NHD * 64], BF16)]
    ysb = sbuf(ctx, tag + "ysb", [128, 16 * 128], BF16)

    ps = psum(ctx, tag + "ps", [128, 6, 512])
    psb = psum(ctx, tag + "psb", [128, 1, 1024], BF16)
    psb2 = psum(ctx, tag + "psb2", [128, 1, 1024], BF16)

    @ctx.block.sync
    def _(s):
        for t_, srcv in ((abias, dt_bias), (eab, a_log), (id32, ident32),
                         (drep, d_rep), (maskt, scan_mask)):
            ii = s.dma_start(out=t_[:, :], in_=srcv[:, :])
            inc(ctx, ii, S("ld"), 16)

    @ctx.block.gpsimd
    def _(g):
        ii = g.dma_start(out=idbf[:, :], in_=ident_f[:, :])
        inc(ctx, ii, S("gld"), 16)
        ii = g.dma_start(out=dtp[:, :], in_=xz_d[4224:4352, :])
        inc(ctx, ii, S("gld"), 16)

    b_ld = cnt(ctx, S("ld"))
    b_gld = cnt(ctx, S("gld"))

    @ctx.block.vector
    def _(v):
        wait(v, ctx, S("ld"), b_ld)
        wait(v, ctx, S("gld"), b_gld)
        v.memset(zeros32[:, :], 0.0)
        v.memset(st_[0][:, :], 0.0)
        v.memset(ones32f[:, :], 1.0)
        ii = v.tensor_scalar_add(d0[:, :], dtp[0:32, :], abias[:, 0:1])
        inc(ctx, ii, S("ev"), 1)

    @ctx.block.scalar
    def _(sce):
        wait(sce, ctx, S("ev"))
        sce.activation(out=eab[:, :], in_=eab[:, :], func=AF.Exp)
        sce.activation(out=tmp[:, :], in_=d0[:, :], func=AF.Abs)
        ii = sce.activation(out=tmp[:, :], in_=tmp[:, :], func=AF.Exp, scale=-1.0)
        inc(ctx, ii, S("sc"), 1)

    @ctx.block.vector
    def _(v):
        wait(v, ctx, S("sc"))
        v.tensor_scalar_add(tmp[:, :], tmp[:, :], 1.0)
        ii = v.tensor_scalar_max(dt_[:, :], d0[:, :], 0.0)
        inc(ctx, ii, S("ev"), 1)

    @ctx.block.scalar
    def _(sce):
        wait(sce, ctx, S("ev"))
        ii = sce.activation(out=tmp[:, :], in_=tmp[:, :], func=AF.Ln)
        inc(ctx, ii, S("sc"), 1)

    @ctx.block.vector
    def _(v):
        wait(v, ctx, S("sc"))
        v.tensor_add(dt_[:, :], dt_[:, :], tmp[:, :])
        ii = v.tensor_scalar(out=tmp[:, :], in0=dt_[:, :], scalar1=eab[:, 0:1],
                             scalar2=-1.0, op0=ALU.mult, op1=ALU.mult)
        inc(ctx, ii, S("ev"), 1)

    @ctx.block.scalar
    def _(sce):
        wait(sce, ctx, S("ev"))
        ii = sce.activation(out=d0[:, :], in_=dt_[:, :], func=AF.Ln)
        inc(ctx, ii, S("sc"), 1)

    @ctx.block.vector
    def _(v):
        wait(v, ctx, S("sc"))
        for c in range(NCK):
            v.tensor_tensor_scan(scum[:, c * 128:(c + 1) * 128],
                                 tmp[:, c * 128:(c + 1) * 128], zeros32[:, :],
                                 0.0, ALU.add, ALU.add)
        ii = v.tensor_sub(qq[:, :], d0[:, :], scum[:, :])
        inc(ctx, ii, S("ev"), 1)

    prep_ev = cnt(ctx, S("ev"))

    B = dict(wl=cnt(ctx, "wl"), xca=cnt(ctx, "xca"), xcb=cnt(ctx, "xcb"),
             wf=cnt(ctx, "wf"), mm=cnt(ctx, "mm"), sa=cnt(ctx, "sa"),
             ex=cnt(ctx, "ex"), ac=cnt(ctx, "ac"), bc=cnt(ctx, "bc"),
             my=cnt(ctx, "my"), ms=cnt(ctx, "ms"), eg=cnt(ctx, "eg"),
             ew=cnt(ctx, "ew"), ey=cnt(ctx, "ey"), st=cnt(ctx, "st"))

    def w8(e, s_, rel):
        if rel > 0:
            e.wait_ge(sm(s_), B[s_] + rel)

    for c in range(NCK):
        cb_ = c % 2
        xv = xcv[:, cb_ * 17 * 128:(cb_ + 1) * 17 * 128]
        xt = xct[:, cb_ * 17 * 128:(cb_ + 1) * 17 * 128]
        br_ = brow[:, cb_ * NHD * 128:(cb_ + 1) * NHD * 128]

        @ctx.block.sync
        def _(s, c=c, xv=xv):
            if c >= 2:
                w8(s, "ey", 16 * (c - 1))
                w8(s, "ms", 32 * (c - 1))
                w8(s, "my", 32 * (c - 1))
            ii = s.dma_start(
                out=xv.rearrange("p (r q) -> p r q", q=128),
                in_=xconv_t[c].rearrange("r p q -> p r q"))
            inc(ctx, ii, S("xca" if c % 2 == 0 else "xcb"), 16)

        @ctx.block.vector
        def _(v, c=c, br_=br_):
            if c == 0:
                wait(v, ctx, S("ev"), prep_ev)
            if c >= 2:
                w8(v, "bc", 32 * (c - 1))
            for h in range(NHD):
                lm = v.tensor_scalar(
                    out=br_[:, h * 128:(h + 1) * 128],
                    in0=scum[:, c * 128:(c + 1) * 128],
                    scalar1=id32[:, h:h + 1], scalar2=None, op0=ALU.mult)
            inc(ctx, lm, S("ac"), 1)

        @ctx.block.tensor
        def _(t, c=c, xv=xv):
            if c == 0:
                wait(t, ctx, S("ev"), prep_ev)
            w8(t, "sa", 33 * c)
            w8(t, "ew", 32 * c)
            ii = t.transpose(ps[:, 0, 0:32], qq[:, c * 128:(c + 1) * 128],
                             id32[:, :])
            inc(ctx, ii, S("mm"), 1)
            if c % 2 == 0:
                t.wait_ge(sm("xca"), B["xca"] + 16 * (c // 2 + 1))
            else:
                t.wait_ge(sm("xcb"), B["xcb"] + 16 * (c // 2 + 1))
            for r in range(17):
                pb = psb if r % 2 == 0 else psb2
                rel = 19 * c + r if r >= 2 else (19 * c - 1 + r if c >= 1 else 0)
                w8(t, "ex", rel)
                ii = t.transpose(pb[:, 0, 0:128], xv[:, r * 128:(r + 1) * 128],
                                 idbf[:, :])
                inc(ctx, ii, S("mm"), 1)
            w8(t, "sa", 33 * c + 1)
            w8(t, "ex", 19 * c + 1)
            ii = t.matmul(ps[:, 0, 0:128], xv[0:64, 16 * 128:16 * 128 + 128],
                          cc0[:, :], start=True, stop=True)
            inc(ctx, ii, S("mm"), 1)

        @ctx.block.scalar
        def _(sce, c=c):
            w8(sce, "mm", 19 * c + 1)
            ii = sce.activation(out=qt[:, :], in_=ps[:, 0, 0:32], func=AF.Copy)
            inc(ctx, ii, S("sa"), 1)

        @ctx.block.vector
        def _(v, c=c, xv=xv, xt=xt):
            if c % 2 == 0:
                v.wait_ge(sm("xca"), B["xca"] + 16 * (c // 2 + 1))
            else:
                v.wait_ge(sm("xcb"), B["xcb"] + 16 * (c // 2 + 1))
            w8(v, "mm", 19 * c)
            ii = v.tensor_copy(cc0[:, :], xv[64:128, 16 * 128:17 * 128])
            inc(ctx, ii, S("ex"), 1)
            for r in range(17):
                pb = psb if r % 2 == 0 else psb2
                w8(v, "mm", 19 * c + 1 + (r + 1))
                ii = v.tensor_copy(xt[:, r * 128:(r + 1) * 128], pb[:, 0, 0:128])
                inc(ctx, ii, S("ex"), 1)
            w8(v, "mm", 19 * c + 19)
            ii = v.tensor_tensor(ptm[:, :], ps[:, 0, 0:128], maskt[:, :], ALU.mult)
            inc(ctx, ii, S("ex"), 1)

        @ctx.block.tensor
        def _(t, c=c, xv=xv, xt=xt, br_=br_):
            def emit_bc(h):
                w8(t, "ac", c + 1)
                w8(t, "sa", 33 * c + 1 + h - 1)
                ii = t.matmul(ps[:, 1 + h % 2, 0:128], ones32f[:, :],
                              br_[:, h * 128:(h + 1) * 128], start=True, stop=True)
                inc(ctx, ii, S("bc"), 1)

            emit_bc(0)
            for h in range(NHD):
                if h + 1 < NHD:
                    emit_bc(h + 1)
                pr = h // 2
                yb = 3 + pr % 2
                w8(t, "eg", 32 * c + h + 1)
                if h % 2 == 0 and pr >= 2:
                    w8(t, "ey", 16 * c + pr - 1)
                if c >= 1:
                    w8(t, "ew", 32 * (c - 1) + h + 1)
                t.matmul(ps[:, yb, 0:128][(h % 2) * 64:(h % 2) * 64 + 64, :],
                         xt[:, (h // 2) * 128 + (h % 2) * 64:(h // 2) * 128 + (h % 2) * 64 + 64],
                         gt[:, (h % 2) * 128:(h % 2) * 128 + 128],
                         start=True, stop=False)
                ii = t.matmul(ps[:, yb, 0:128][(h % 2) * 64:(h % 2) * 64 + 64, :],
                              st_[c % 2][:, h * 64:(h + 1) * 64],
                              cce[:, (h % 2) * 128:(h % 2) * 128 + 128],
                              start=False, stop=True)
                inc(ctx, ii, S("my"), 1)
                sb_ = 5 if h % 2 == 0 else 0
                if h % 2 == 1:
                    w8(t, "ex", 19 * (c + 1))
                w8(t, "ew", 32 * c + h - 1)
                ii = t.matmul(ps[0:64, sb_, 0:64],
                              wb[:, (h % 2) * 64:(h % 2) * 64 + 64],
                              xt[:, (h // 2) * 128 + (h % 2) * 64:(h // 2) * 128 + (h % 2) * 64 + 64],
                              start=True, stop=True)
                inc(ctx, ii, S("ms"), 1)

        @ctx.block.scalar
        def _(sce, c=c):
            for h in range(NHD):
                hb = h % 2
                w8(sce, "bc", 32 * c + h + 1)
                w8(sce, "wf", 32 * c + h + 1)
                w8(sce, "eg", 32 * c + h - 1)
                w8(sce, "ew", 32 * c + h - 1)
                sce.activation(out=mt_t[:, hb * 128:(hb + 1) * 128],
                               in_=mcl[:, hb * 128:(hb + 1) * 128], func=AF.Exp)
                sce.activation(out=wcol[:, hb:hb + 1],
                               in_=ps[:, 1 + hb, 127:128], func=AF.Exp,
                               bias=qt[:, h:h + 1])
                ii = sce.activation(out=ebc[:, hb * 128:(hb + 1) * 128],
                                    in_=ps[0:64, 1 + hb, 0:128], func=AF.Exp)
                inc(ctx, ii, S("sa"), 1)

        def emit_clamp(v, h, c):
            hb = h % 2
            w8(v, "bc", 32 * c + h + 1)
            if h >= 2:
                w8(v, "sa", 33 * c + 1 + h - 1)
            ii = v.tensor_scalar(out=mcl[:, hb * 128:(hb + 1) * 128],
                                 in0=ps[:, 1 + hb, 0:128],
                                 scalar1=qt[:, h:h + 1], scalar2=10.0,
                                 op0=ALU.add, op1=ALU.min)
            inc(ctx, ii, S("wf"), 1)

        @ctx.block.vector
        def _(v, c=c, xv=xv, xt=xt):
            emit_clamp(v, 0, c)
            for h in range(NHD):
                hb = h % 2
                if h + 1 < NHD:
                    emit_clamp(v, h + 1, c)
                w8(v, "sa", 33 * c + 1 + h + 1)
                w8(v, "my", 32 * c + h - 1)
                w8(v, "ms", 32 * c + h - 1)
                v.tensor_tensor(gt[:, hb * 128:(hb + 1) * 128],
                                mt_t[:, hb * 128:(hb + 1) * 128],
                                ptm[:, :], ALU.mult)
                v.tensor_tensor(cce[:, hb * 128:(hb + 1) * 128],
                                cc0[:, :],
                                ebc[:, hb * 128:(hb + 1) * 128], ALU.mult)
                ii = v.tensor_scalar(
                    out=wb[:, hb * 64:(hb + 1) * 64],
                    in0=xt[:, 16 * 128:16 * 128 + 64],
                    scalar1=wcol[:, hb:hb + 1],
                    scalar2=None, op0=ALU.mult)
                inc(ctx, ii, S("eg"), 1)
                sb_ = 5 if h % 2 == 0 else 0
                w8(v, "ms", 32 * c + h + 1)
                ii = v.scalar_tensor_tensor(
                    out=st_[(c + 1) % 2][:, h * 64:(h + 1) * 64],
                    in0=st_[c % 2][:, h * 64:(h + 1) * 64],
                    scalar=ebc[:, hb * 128 + 127:hb * 128 + 128],
                    in1=ps[0:64, sb_, 0:64],
                    op0=ALU.mult, op1=ALU.add)
                inc(ctx, ii, S("ew"), 1)
                if h % 2 == 1:
                    pr = h // 2
                    yb = 3 + pr % 2
                    w8(v, "my", 32 * c + h + 1)
                    if pr == 0:
                        w8(v, "st", 16 * c)
                    ii = v.scalar_tensor_tensor(
                        out=ysb[:, pr * 128:(pr + 1) * 128],
                        in0=xv[:, pr * 128:(pr + 1) * 128],
                        scalar=drep[:, pr:pr + 1],
                        in1=ps[:, yb, 0:128],
                        op0=ALU.mult, op1=ALU.add)
                    inc(ctx, ii, S("ey"), 1)

        @ctx.block.sync
        def _(s, c=c):
            w8(s, "ey", 16 * (c + 1))
            ii = s.dma_start(
                out=y_d.rearrange("(r p) t -> p r t", p=128)[:, :, c * 128:(c + 1) * 128],
                in_=ysb[:, :].rearrange("p (r q) -> p r q", q=128))
            inc(ctx, ii, S("st"), 16)

    fence_dma(ctx, "sync", S("st"))
    barrier(ctx)
    end_scope(ctx)


def phase_gate_mnorm(ctx, y_d, xz_d, mnorm_w, yn_d, tag="F"):
    begin_scope(ctx)
    S = lambda n: n
    sm = lambda n: ctx.sems[n]

    zb = sbuf(ctx, tag + "z", [128, 2 * TT], BF16)
    yb = sbuf(ctx, tag + "y", [128, 2 * TT], BF16)
    szb = sbuf(ctx, tag + "sz", [128, 2 * TT], BF16)
    yz = sbuf(ctx, tag + "yz", [128, NCH * TT], BF16)
    sq = sbuf(ctx, tag + "sq", [128, 2 * TT], BF16)
    var = sbuf(ctx, tag + "var", [128, TT], F32)
    sd = sbuf(ctx, tag + "sd", [128, TT], F32)
    r2 = sbuf(ctx, tag + "r2", [128, TT], F32)
    tmp = sbuf(ctx, tag + "tmp", [128, TT], F32)
    xo = sbuf(ctx, tag + "xo", [128, 2 * TT], BF16)
    wt = sbuf(ctx, tag + "wt", [128, NCH], F32)
    ps = psum(ctx, tag + "ps", [128, 8, 512])
    ones = ctx.ones

    @ctx.block.sync
    def _(s):
        ii = s.dma_start(out=wt[:, :], in_=mnorm_w[:, :])
        inc(ctx, ii, S("ld"), 16)

    for tt in range(NTT):
        t0 = tt * TT
        b_ld = cnt(ctx, S("ld"))
        b_sc = cnt(ctx, S("sc"))
        b_ev = cnt(ctx, S("ev"))
        b_mm = cnt(ctx, S("mm"))
        b_st = cnt(ctx, S("st"))

        b_lp = (cnt(ctx, S("lda")), cnt(ctx, S("ldb")))

        @ctx.block.sync
        def _(s, t0=t0, b_ev=b_ev):
            for c in range(NCH):
                if c >= 2:
                    s.wait_ge(sm("ev"), b_ev + c - 1)
                sn = "lda" if c % 2 == 0 else "ldb"
                ii = s.dma_start(out=zb[:, (c % 2) * TT:(c % 2 + 1) * TT],
                                 in_=xz_d[c * 128:(c + 1) * 128, t0:t0 + TT])
                inc(ctx, ii, S(sn), 16)
                ii = s.dma_start(out=yb[:, (c % 2) * TT:(c % 2 + 1) * TT],
                                 in_=y_d[c * 128:(c + 1) * 128, t0:t0 + TT])
                inc(ctx, ii, S(sn), 16)

        @ctx.block.scalar
        def _(sce, b_lp=b_lp, b_ev=b_ev):
            for c in range(NCH):
                sn = "lda" if c % 2 == 0 else "ldb"
                sce.wait_ge(sm(sn), b_lp[c % 2] + 32 * (c // 2 + 1))
                if c >= 2:
                    sce.wait_ge(sm("ev"), b_ev + c - 1)
                hb = (c % 2) * TT
                ii = sce.activation(out=szb[:, hb:hb + TT], in_=zb[:, hb:hb + TT],
                                    func=AF.Silu)
                inc(ctx, ii, S("sc"), 1)

        @ctx.block.vector
        def _(v, b_lp=b_lp, b_sc=b_sc):
            for c in range(NCH):
                sn = "lda" if c % 2 == 0 else "ldb"
                v.wait_ge(sm(sn), b_lp[c % 2] + 32 * (c // 2 + 1))
                v.wait_ge(sm("sc"), b_sc + c + 1)
                hb = (c % 2) * TT
                ii = v.tensor_mul(yz[:, c * TT:(c + 1) * TT], yb[:, hb:hb + TT],
                                  szb[:, hb:hb + TT])
                inc(ctx, ii, S("ev"), 1)

        b_sc2 = cnt(ctx, S("sc"))

        @ctx.block.scalar
        def _(sce, b_ev=b_ev, b_mm=b_mm):
            for c in range(NCH):
                sce.wait_ge(sm("ev"), b_ev + c + 1)
                if c >= 2:
                    sce.wait_ge(sm("mm"), b_mm + c - 1)
                hb = (c % 2) * TT
                ii = sce.activation(out=sq[:, hb:hb + TT],
                                    in_=yz[:, c * TT:(c + 1) * TT], func=AF.Square)
                inc(ctx, ii, S("sc"), 1)

        @ctx.block.tensor
        def _(t, b_sc2=b_sc2, tt=tt):
            wait(t, ctx, S("ev"))
            for c in range(NCH):
                t.wait_ge(sm("sc"), b_sc2 + c + 1)
                hb = (c % 2) * TT
                for sub in range(NSUB):
                    lm = t.matmul(ps[:, sub, :], ones[:, :],
                                  sq[:, hb + sub * 512:hb + (sub + 1) * 512],
                                  start=(c == 0), stop=(c == NCH - 1))
                inc(ctx, lm, S("mm"), 1)

        @ctx.block.vector
        def _(v):
            wait(v, ctx, S("mm"))
            for sub in range(NSUB):
                lm = v.tensor_scalar(
                    out=var[:, sub * 512:(sub + 1) * 512], in0=ps[:, sub, :],
                    scalar1=1.0 / D, scalar2=EPS, op0=ALU.mult, op1=ALU.add)
            inc(ctx, lm, S("ev"), 1)

        @ctx.block.scalar
        def _(sce):
            wait(sce, ctx, S("ev"))
            ii = sce.activation(out=sd[:, :], in_=var[:, :], func=AF.Sqrt)
            inc(ctx, ii, S("sc"), 1)

        @ctx.block.vector
        def _(v):
            wait(v, ctx, S("sc"))
            ii = v.reciprocal(r2[:, :], sd[:, :])
            inc(ctx, ii, S("ev"), 1)

        sc_b3 = cnt(ctx, S("sc"))
        ev_b3 = cnt(ctx, S("ev"))
        st_b3 = cnt(ctx, S("st"))

        @ctx.block.vector
        def _(v, ev_b3=ev_b3, sc_b3=sc_b3):
            wait(v, ctx, S("ev"), ev_b3)
            for c in range(NCH):
                if c >= 1:
                    v.wait_ge(sm("sc"), sc_b3 + c)
                ii = v.tensor_mul(tmp[:, :], yz[:, c * TT:(c + 1) * TT], r2[:, :])
                inc(ctx, ii, S("ev"), 1)

        @ctx.block.scalar
        def _(sce, ev_b3=ev_b3, st_b3=st_b3):
            wait(sce, ctx, S("ld"))
            for c in range(NCH):
                sce.wait_ge(sm("ev"), ev_b3 + c + 1)
                if c >= 2:
                    sce.wait_ge(sm("st"), st_b3 + 16 * c)
                hb = (c % 2) * TT
                ii = sce.activation(out=xo[:, hb:hb + TT], in_=tmp[:, :],
                                    func=AF.Copy, scale=wt[:, c:c + 1])
                inc(ctx, ii, S("sc"), 1)

        @ctx.block.sync
        def _(s, t0=t0, sc_b3=sc_b3):
            for c in range(NCH):
                s.wait_ge(sm("sc"), sc_b3 + c + 1)
                hb = (c % 2) * TT
                ii = s.dma_start(out=yn_d[c * 128:(c + 1) * 128, t0:t0 + TT],
                                 in_=xo[:, hb:hb + TT])
                inc(ctx, ii, S("st"), 16)

    fence_dma(ctx, "sync", S("st"))
    barrier(ctx)
    end_scope(ctx)


def phase_rms(ctx, src, src_f32, w_rms, out, tag="G"):
    begin_scope(ctx)
    S = lambda n: n
    sm = lambda n: ctx.sems[n]

    act = sbuf(ctx, tag + "act", [128, NCH * TT], BF16)
    sq = sbuf(ctx, tag + "sq", [128, 2 * TT], BF16)
    var = sbuf(ctx, tag + "var", [128, TT], F32)
    sd = sbuf(ctx, tag + "sd", [128, TT], F32)
    r2 = sbuf(ctx, tag + "r2", [128, TT], F32)
    tmp = sbuf(ctx, tag + "tmp", [128, TT], F32)
    xo = sbuf(ctx, tag + "xo", [128, 2 * TT], BF16)
    wt = sbuf(ctx, tag + "wt", [128, NCH], F32)
    ps = psum(ctx, tag + "ps", [128, 8, 512])
    ones = ctx.ones

    @ctx.block.sync
    def _(s):
        ii = s.dma_start(out=wt[:, :], in_=w_rms[:, :])
        inc(ctx, ii, S("ld"), 16)

    for tt in range(NTT):
        t0 = tt * TT
        b_ld = cnt(ctx, S("ld"))
        b_sc = cnt(ctx, S("sc"))
        b_mm = cnt(ctx, S("mm"))
        b_st = cnt(ctx, S("st"))

        ldeng = "gpsimd" if src_f32 else "sync"
        ldsm = "gld" if src_f32 else "ld"
        sec = getattr(ctx.block, ldeng)

        @sec
        def _(g, t0=t0):
            if t0 > 0:
                wait(g, ctx, S("st"))
            for c in range(NCH):
                ii = g.dma_start(out=act[:, c * TT:(c + 1) * TT],
                                 in_=src[c * 128:(c + 1) * 128, t0:t0 + TT])
                inc(ctx, ii, S(ldsm), 16)

        ld_all = cnt(ctx, S(ldsm))

        @ctx.block.scalar
        def _(sce, ld_all=ld_all, b_mm=b_mm):
            wait(sce, ctx, S("ld"))
            sce.wait_ge(sm(ldsm), ld_all)
            for c in range(NCH):
                if c >= 2:
                    sce.wait_ge(sm("mm"), b_mm + c - 1)
                hb = (c % 2) * TT
                ii = sce.activation(out=sq[:, hb:hb + TT],
                                    in_=act[:, c * TT:(c + 1) * TT], func=AF.Square)
                inc(ctx, ii, S("sc"), 1)

        @ctx.block.tensor
        def _(t, b_sc=b_sc, ld_all=ld_all):
            wait(t, ctx, S("ev"))
            t.wait_ge(sm(ldsm), ld_all)
            for c in range(NCH):
                t.wait_ge(sm("sc"), b_sc + c + 1)
                hb = (c % 2) * TT
                for sub in range(NSUB):
                    lm = t.matmul(ps[:, sub, :], ones[:, :],
                                  sq[:, hb + sub * 512:hb + (sub + 1) * 512],
                                  start=(c == 0), stop=(c == NCH - 1))
                inc(ctx, lm, S("mm"), 1)

        @ctx.block.vector
        def _(v):
            wait(v, ctx, S("mm"))
            for sub in range(NSUB):
                lm = v.tensor_scalar(
                    out=var[:, sub * 512:(sub + 1) * 512], in0=ps[:, sub, :],
                    scalar1=1.0 / D, scalar2=EPS, op0=ALU.mult, op1=ALU.add)
            inc(ctx, lm, S("ev"), 1)

        @ctx.block.scalar
        def _(sce):
            wait(sce, ctx, S("ev"))
            ii = sce.activation(out=sd[:, :], in_=var[:, :], func=AF.Sqrt)
            inc(ctx, ii, S("sc"), 1)

        @ctx.block.vector
        def _(v):
            wait(v, ctx, S("sc"))
            ii = v.reciprocal(r2[:, :], sd[:, :])
            inc(ctx, ii, S("ev"), 1)

        sc_b3 = cnt(ctx, S("sc"))
        ev_b3 = cnt(ctx, S("ev"))
        st_b3 = cnt(ctx, S("st"))

        @ctx.block.vector
        def _(v, ev_b3=ev_b3, sc_b3=sc_b3):
            wait(v, ctx, S("ev"), ev_b3)
            for c in range(NCH):
                if c >= 1:
                    v.wait_ge(sm("sc"), sc_b3 + c)
                ii = v.tensor_mul(tmp[:, :], act[:, c * TT:(c + 1) * TT], r2[:, :])
                inc(ctx, ii, S("ev"), 1)

        @ctx.block.scalar
        def _(sce, ev_b3=ev_b3, st_b3=st_b3):
            for c in range(NCH):
                sce.wait_ge(sm("ev"), ev_b3 + c + 1)
                if c >= 2:
                    sce.wait_ge(sm("st"), st_b3 + 16 * c)
                hb = (c % 2) * TT
                ii = sce.activation(out=xo[:, hb:hb + TT], in_=tmp[:, :],
                                    func=AF.Copy, scale=wt[:, c:c + 1])
                inc(ctx, ii, S("sc"), 1)

        @ctx.block.sync
        def _(s, t0=t0, sc_b3=sc_b3):
            for c in range(NCH):
                s.wait_ge(sm("sc"), sc_b3 + c + 1)
                hb = (c % 2) * TT
                ii = s.dma_start(out=out[c * 128:(c + 1) * 128, t0:t0 + TT],
                                 in_=xo[:, hb:hb + TT])
                inc(ctx, ii, S("st"), 16)

    fence_dma(ctx, "sync", S("st"))
    barrier(ctx)
    end_scope(ctx)


def phase_mlp_in(ctx, un_d, w_mlp_in, swiglu_d, tag="H"):
    begin_scope(ctx)
    S = lambda n: n
    sm = lambda n: ctx.sems[n]
    KT = 16
    NP = 64

    a_sb = sbuf(ctx, tag + "a", [128, KT * TT], BF16)
    w_sb = sbuf(ctx, tag + "w", [128, 4 * KT * 128], BF16)
    sg = sbuf(ctx, tag + "sg", [128, 2 * TT], BF16)
    swb = sbuf(ctx, tag + "sw", [128, 2 * TT], BF16)
    ps = psum(ctx, tag + "ps", [128, 8, 512])

    b_wl2 = (cnt(ctx, S("wla")), cnt(ctx, S("wlb")))
    b_mm = cnt(ctx, S("mm"))
    b_bc = cnt(ctx, S("bc"))
    b_sc = cnt(ctx, S("sc"))
    b_ev = cnt(ctx, S("ev"))
    b_st = cnt(ctx, S("st"))

    for tt in range(NTT):
        t0 = tt * TT

        @ctx.block.sync
        def _(s, tt=tt, t0=t0):
            if tt > 0:
                wait(s, ctx, S("mm"))
            for c in range(KT):
                ii = s.dma_start(out=a_sb[:, c * TT:(c + 1) * TT],
                                 in_=un_d[c * 128:(c + 1) * 128, t0:t0 + TT])
                inc(ctx, ii, S("ld"), 16)

        ld_hi = cnt(ctx, S("ld"))

        @ctx.block.gpsimd
        def _(g, tt=tt):
            for mi in range(NP):
                gi = tt * NP + mi
                if gi >= 2:
                    g.wait_ge(sm("mm"), b_mm + gi - 1)
                buf = (gi % 2) * 2
                sn = "wla" if gi % 2 == 0 else "wlb"
                for j, mrow in enumerate((mi, 64 + mi)):
                    ii = g.dma_start(
                        out=w_sb[:, (buf + j) * KT * 128:(buf + j + 1) * KT * 128].rearrange(
                            "p (k q) -> p k q", q=128),
                        in_=w_mlp_in[mrow].rearrange("k p q -> p k q"))
                    inc(ctx, ii, S(sn), 16)

        @ctx.block.tensor
        def _(t, tt=tt, ld_hi=ld_hi):
            t.wait_ge(sm("ld"), ld_hi)
            for mi in range(NP):
                gi = tt * NP + mi
                t.wait_ge(sm("wla" if gi % 2 == 0 else "wlb"),
                          b_wl2[gi % 2] + 32 * (gi // 2 + 1))
                buf = (gi % 2) * 2
                if gi >= 1:
                    t.wait_ge(sm("sc"), b_sc + gi)
                for k in range(KT):
                    for sub in range(NSUB):
                        lm = t.matmul(
                            ps[:, sub, :],
                            w_sb[:, buf * KT * 128 + k * 128:buf * KT * 128 + (k + 1) * 128],
                            a_sb[:, k * TT + sub * 512:k * TT + (sub + 1) * 512],
                            start=(k == 0), stop=(k == KT - 1))
                inc(ctx, lm, S("bc"), 1)
                if gi >= 1:
                    t.wait_ge(sm("ev"), b_ev + gi)
                for k in range(KT):
                    for sub in range(NSUB):
                        lm = t.matmul(
                            ps[:, 4 + sub, :],
                            w_sb[:, (buf + 1) * KT * 128 + k * 128:(buf + 1) * KT * 128 + (k + 1) * 128],
                            a_sb[:, k * TT + sub * 512:k * TT + (sub + 1) * 512],
                            start=(k == 0), stop=(k == KT - 1))
                inc(ctx, lm, S("mm"), 1)

        @ctx.block.scalar
        def _(sce, tt=tt):
            for mi in range(NP):
                gi = tt * NP + mi
                sce.wait_ge(sm("bc"), b_bc + gi + 1)
                if gi >= 2:
                    sce.wait_ge(sm("ev"), b_ev + gi - 1)
                hb = (gi % 2) * TT
                for sub in range(NSUB):
                    lm = sce.activation(out=sg[:, hb + sub * 512:hb + (sub + 1) * 512],
                                        in_=ps[:, sub, :], func=AF.Silu)
                inc(ctx, lm, S("sc"), 1)

        @ctx.block.vector
        def _(v, tt=tt):
            for mi in range(NP):
                gi = tt * NP + mi
                v.wait_ge(sm("sc"), b_sc + gi + 1)
                v.wait_ge(sm("mm"), b_mm + gi + 1)
                if gi >= 2:
                    v.wait_ge(sm("st"), b_st + 16 * gi)
                hb = (gi % 2) * TT
                for sub in range(NSUB):
                    lm = v.tensor_tensor(swb[:, hb + sub * 512:hb + (sub + 1) * 512],
                                         sg[:, hb + sub * 512:hb + (sub + 1) * 512],
                                         ps[:, 4 + sub, :], ALU.mult)
                inc(ctx, lm, S("ev"), 1)

        @ctx.block.sync
        def _(s, tt=tt, t0=t0):
            for mi in range(NP):
                gi = tt * NP + mi
                s.wait_ge(sm("ev"), b_ev + gi + 1)
                hb = (gi % 2) * TT
                ii = s.dma_start(out=swiglu_d[mi * 128:(mi + 1) * 128, t0:t0 + TT],
                                 in_=swb[:, hb:hb + TT])
                inc(ctx, ii, S("st"), 16)

    fence_dma(ctx, "sync", S("st"))
    barrier(ctx)
    end_scope(ctx)


def phase_gather(ctx, side, xn1_d, x1_d, enz_d, xn1m_d, x1m_d, enzm_d,
                 tag="K"):
    begin_scope(ctx)
    S = lambda n: n
    sm = lambda n: ctx.sems[n]

    @ctx.block.gpsimd
    def _(g):
        with g.register(tag + "side") as sr:
            g.reg_load(sr, side[0:1, 0:1])
            off = g.snap(sr, min_val=0, max_val=1)
            for sd, dd, nch in ((xn1_d, xn1m_d, NCH), (x1_d, x1m_d, NCH),
                                (enz_d, enzm_d, 8)):
                for c0 in range(0, nch, 4):
                    xv = sd[c0 * 128:(c0 + 4) * 128, :].rearrange(
                        "(r p) (b s q) -> r p b s q", p=128, s=2, q=128)
                    ii = g.dma_start(
                        out=dd[c0 * 128:(c0 + 4) * 128, :].rearrange(
                            "(r p) (b q) -> r p b q", p=128, q=128),
                        in_=xv[:, :, :, bass.ds(off, 1), :])
                    inc(ctx, ii, S("gst"), 16)

    fence_dma(ctx, "gpsimd", S("gst"))
    barrier(ctx)
    end_scope(ctx)


def phase_proj_rope(ctx, tag, act, w_t, cosT, sinT, out, t_total=T):
    begin_scope(ctx)
    S = lambda n: n
    sm = lambda n: ctx.sems[n]
    KT = 16
    NHD1 = 16
    TTo = TT
    NSUBo = TTo // 512
    ntt = t_total // TTo

    a_sb = sbuf(ctx, tag + "a", [128, KT * TTo], BF16)
    w_sb = sbuf(ctx, tag + "w", [128, 4 * KT * 64], BF16)
    cs_sb = sbuf(ctx, tag + "cs", [64, 2 * TTo], F32)
    sn_sb = sbuf(ctx, tag + "sn", [64, 2 * TTo], F32)
    e1 = sbuf(ctx, tag + "e1", [64, TTo], F32)
    e2 = sbuf(ctx, tag + "e2", [64, TTo], F32)
    o_sb = sbuf(ctx, tag + "o", [64, 4 * TTo], BF16)
    ps = psum(ctx, tag + "ps", [128, 8, 512])

    b_wl2 = (cnt(ctx, S("wla")), cnt(ctx, S("wlb")))
    b_mm = cnt(ctx, S("mm"))
    b_ev = cnt(ctx, S("ev"))
    b_st = cnt(ctx, S("st"))
    b_lp = (cnt(ctx, S("lda")), cnt(ctx, S("ldb")))

    for tt in range(ntt):
        t0 = tt * TTo

        @ctx.block.sync
        def _(s, tt=tt, t0=t0):
            if tt > 0:
                wait(s, ctx, S("mm"))
            for c in range(KT):
                ii = s.dma_start(out=a_sb[:, c * TTo:(c + 1) * TTo],
                                 in_=act[c * 128:(c + 1) * 128, t0:t0 + TTo])
                inc(ctx, ii, S("ld"), 16)
            sn = "lda" if tt % 2 == 0 else "ldb"
            hb = (tt % 2) * TTo
            ii = s.dma_start(out=cs_sb[:, hb:hb + TTo], in_=cosT[:, t0:t0 + TTo])
            inc(ctx, ii, S(sn), 16)
            ii = s.dma_start(out=sn_sb[:, hb:hb + TTo], in_=sinT[:, t0:t0 + TTo])
            inc(ctx, ii, S(sn), 16)

        ld_hi = cnt(ctx, S("ld"))

        @ctx.block.gpsimd
        def _(g, tt=tt):
            for m in range(NHD1):
                gi = tt * NHD1 + m
                if gi >= 2:
                    g.wait_ge(sm("mm"), b_mm + gi - 1)
                buf = (gi % 2) * 2
                sn = "wla" if gi % 2 == 0 else "wlb"
                for half in range(2):
                    ii = g.dma_start(
                        out=w_sb[:, (buf + half) * KT * 64:(buf + half + 1) * KT * 64].rearrange(
                            "p (k q) -> p k q", q=64),
                        in_=w_t[m, half].rearrange("k p q -> p k q"))
                    inc(ctx, ii, S(sn), 16)

        @ctx.block.tensor
        def _(t, tt=tt, ld_hi=ld_hi):
            t.wait_ge(sm("ld"), ld_hi)
            for m in range(NHD1):
                gi = tt * NHD1 + m
                t.wait_ge(sm("wla" if gi % 2 == 0 else "wlb"),
                          b_wl2[gi % 2] + 32 * (gi // 2 + 1))
                buf = (gi % 2) * 2
                if gi >= 1:
                    t.wait_ge(sm("ev"), b_ev + gi)
                for half in range(2):
                    wbase = (buf + half) * KT * 64
                    for k in range(KT):
                        for sub in range(NSUBo):
                            lm = t.matmul(
                                ps[0:64, half * 4 + sub, :],
                                w_sb[:, wbase + k * 64:wbase + (k + 1) * 64],
                                a_sb[:, k * TTo + sub * 512:k * TTo + (sub + 1) * 512],
                                start=(k == 0), stop=(k == KT - 1))
                inc(ctx, lm, S("mm"), 1)

        @ctx.block.vector
        def _(v, tt=tt):
            hb = (tt % 2) * TTo
            sn = "lda" if tt % 2 == 0 else "ldb"
            v.wait_ge(sm(sn), b_lp[tt % 2] + 32 * (tt // 2 + 1))
            for m in range(NHD1):
                gi = tt * NHD1 + m
                v.wait_ge(sm("mm"), b_mm + gi + 1)
                ob = (gi % 2) * 2 * TTo
                if gi >= 2:
                    v.wait_ge(sm("st"), b_st + 32 * gi)
                for sub in range(NSUBo):
                    sl5 = slice(sub * 512, (sub + 1) * 512)
                    pA = ps[0:64, sub, :]
                    pB = ps[0:64, 4 + sub, :]
                    v.tensor_mul(e1[:, sl5], pA, cs_sb[:, hb + sub * 512:hb + (sub + 1) * 512])
                    v.tensor_mul(e2[:, sl5], pB, sn_sb[:, hb + sub * 512:hb + (sub + 1) * 512])
                    v.tensor_sub(o_sb[:, ob + sub * 512:ob + (sub + 1) * 512],
                                 e1[:, sl5], e2[:, sl5])
                    v.tensor_mul(e1[:, sl5], pB, cs_sb[:, hb + sub * 512:hb + (sub + 1) * 512])
                    v.tensor_mul(e2[:, sl5], pA, sn_sb[:, hb + sub * 512:hb + (sub + 1) * 512])
                    lm = v.tensor_add(o_sb[:, ob + TTo + sub * 512:ob + TTo + (sub + 1) * 512],
                                      e1[:, sl5], e2[:, sl5])
                inc(ctx, lm, S("ev"), 1)

        @ctx.block.sync
        def _(s, tt=tt, t0=t0):
            for m in range(NHD1):
                gi = tt * NHD1 + m
                s.wait_ge(sm("ev"), b_ev + gi + 1)
                ob = (gi % 2) * 2 * TTo
                ii = s.dma_start(out=out[m * 128:m * 128 + 64, t0:t0 + TTo],
                                 in_=o_sb[:, ob:ob + TTo])
                inc(ctx, ii, S("st"), 16)
                ii = s.dma_start(out=out[m * 128 + 64:m * 128 + 128, t0:t0 + TTo],
                                 in_=o_sb[:, ob + TTo:ob + 2 * TTo])
                inc(ctx, ii, S("st"), 16)

    fence_dma(ctx, "sync", S("st"))
    barrier(ctx)
    end_scope(ctx)


def phase_matmulA(ctx, tag, act, w_r, out, t_total=T):
    begin_scope(ctx)
    S = lambda n: n
    sm = lambda n: ctx.sems[n]
    KT = 16
    NG = 4

    a_sb = sbuf(ctx, tag + "a", [128, KT * TT], BF16)
    w_sb = sbuf(ctx, tag + "w", [128, KT * D], BF16)
    o_sb = sbuf(ctx, tag + "o", [128, 2 * 512], BF16)
    ps = psum(ctx, tag + "ps", [128, 2, 512])

    b_mm = cnt(ctx, S("mm"))
    b_ev = cnt(ctx, S("ev"))
    b_st = cnt(ctx, S("st"))

    @ctx.block.gpsimd
    def _(g):
        for k in range(KT):
            ii = g.dma_start(out=w_sb[:, k * D:(k + 1) * D],
                             in_=w_r[k * 128:(k + 1) * 128, :])
            inc(ctx, ii, S("wla"), 16)

    wl_hi = cnt(ctx, S("wla"))
    ntt = t_total // TT
    for tt in range(ntt):
        t0 = tt * TT

        @ctx.block.sync
        def _(s, tt=tt, t0=t0):
            if tt > 0:
                wait(s, ctx, S("mm"))
            for c in range(KT):
                ii = s.dma_start(out=a_sb[:, c * TT:(c + 1) * TT],
                                 in_=act[c * 128:(c + 1) * 128, t0:t0 + TT])
                inc(ctx, ii, S("ld"), 16)

        ld_hi = cnt(ctx, S("ld"))

        @ctx.block.tensor
        def _(t, tt=tt, ld_hi=ld_hi):
            t.wait_ge(sm("ld"), ld_hi)
            t.wait_ge(sm("wla"), wl_hi)
            for tc in range(TT // 128):
                for ng in range(NG):
                    gi = (tt * (TT // 128) + tc) * NG + ng
                    if gi >= 2:
                        t.wait_ge(sm("ev"), b_ev + gi - 1)
                    for k in range(KT):
                        lm = t.matmul(
                            ps[:, gi % 2, :],
                            a_sb[:, k * TT + tc * 128:k * TT + (tc + 1) * 128],
                            w_sb[:, k * D + ng * 512:k * D + (ng + 1) * 512],
                            start=(k == 0), stop=(k == KT - 1))
                    inc(ctx, lm, S("mm"), 1)

        @ctx.block.vector
        def _(v, tt=tt):
            for tc in range(TT // 128):
                for ng in range(NG):
                    gi = (tt * (TT // 128) + tc) * NG + ng
                    v.wait_ge(sm("mm"), b_mm + gi + 1)
                    if gi >= 2:
                        v.wait_ge(sm("st"), b_st + 16 * gi)
                    lm = v.tensor_copy(o_sb[:, (gi % 2) * 512:(gi % 2 + 1) * 512],
                                       ps[:, gi % 2, :])
                    inc(ctx, lm, S("ev"), 1)

        @ctx.block.sync
        def _(s, tt=tt, t0=t0):
            for tc in range(TT // 128):
                for ng in range(NG):
                    gi = (tt * (TT // 128) + tc) * NG + ng
                    s.wait_ge(sm("ev"), b_ev + gi + 1)
                    ii = s.dma_start(
                        out=out[t0 + tc * 128:t0 + (tc + 1) * 128,
                                ng * 512:(ng + 1) * 512],
                        in_=o_sb[:, (gi % 2) * 512:(gi % 2 + 1) * 512])
                    inc(ctx, ii, S("st"), 16)

    fence_dma(ctx, "sync", S("st"))
    barrier(ctx)
    end_scope(ctx)


def phase_spath(ctx, xn1_d, ws1T, ws2T, bs2c, bs2b, es_d, enz_d, estm_d,
                tag="N"):
    begin_scope(ctx)
    S = lambda n: n
    sm = lambda n: ctx.sems[n]
    M8 = 8

    a_sb = sbuf(ctx, tag + "a", [128, 16 * TT], BF16)
    w1 = sbuf(ctx, tag + "w1", [128, 16 * 16], BF16)
    w2 = sbuf(ctx, tag + "w2", [16, 1024], BF16)
    s1b = sbuf(ctx, tag + "s1", [16, T], BF16)
    bs2t = sbuf(ctx, tag + "b2", [128, 8], F32)
    bs2bc = sbuf(ctx, tag + "b2b", [128, 1024], F32)
    esf = sbuf(ctx, tag + "esf", [128, T], F32)
    csf = sbuf(ctx, tag + "csf", [128, T], F32)
    eb = sbuf(ctx, tag + "eb", [128, T], BF16)
    etm = sbuf(ctx, tag + "etm", [128, 2 * 1024], F32)
    etb = sbuf(ctx, tag + "etb", [128, 2 * 1024], BF16)
    ps = psum(ctx, tag + "ps", [128, 8, 512])

    b_ld = cnt(ctx, S("ld"))
    b_gld = cnt(ctx, S("gld"))

    @ctx.block.sync
    def _(s):
        ii = s.dma_start(out=bs2t[:, :], in_=bs2c[:, :])
        inc(ctx, ii, S("ld"), 16)
        ii = s.dma_start(out=bs2bc[:, :], in_=bs2b[:, :])
        inc(ctx, ii, S("ld"), 16)

    @ctx.block.gpsimd
    def _(g):
        ii = g.dma_start(out=w1[:, :].rearrange("p (k q) -> p k q", q=16),
                         in_=ws1T.rearrange("(k p) q -> p k q", p=128))
        inc(ctx, ii, S("gld"), 16)
        ii = g.dma_start(out=w2[:, :], in_=ws2T[:, :])
        inc(ctx, ii, S("gld"), 16)

    for tt in range(NTT):
        t0 = tt * TT

        @ctx.block.sync
        def _(s, t0=t0, tt=tt):
            if tt > 0:
                wait(s, ctx, S("mm"))
            for c in range(16):
                ii = s.dma_start(out=a_sb[:, c * TT:(c + 1) * TT],
                                 in_=xn1_d[c * 128:(c + 1) * 128, t0:t0 + TT])
                inc(ctx, ii, S("ld"), 16)

        ld_hi = cnt(ctx, S("ld"))
        gld_hi = cnt(ctx, S("gld"))
        b_mm = cnt(ctx, S("mm"))

        @ctx.block.tensor
        def _(t, tt=tt, ld_hi=ld_hi, gld_hi=gld_hi, b_mm=b_mm):
            t.wait_ge(sm("ld"), ld_hi)
            t.wait_ge(sm("gld"), gld_hi)
            wait(t, ctx, S("ev"))
            for sub in range(NSUB):
                for k in range(16):
                    lm = t.matmul(ps[0:16, sub, :],
                                  w1[:, k * 16:(k + 1) * 16],
                                  a_sb[:, k * TT + sub * 512:k * TT + (sub + 1) * 512],
                                  start=(k == 0), stop=(k == 15))
            inc(ctx, lm, S("mm"), 1)

        @ctx.block.vector
        def _(v, tt=tt, t0=t0, b_mm=b_mm):
            v.wait_ge(sm("mm"), b_mm + 1)
            for sub in range(NSUB):
                lm = v.tensor_copy(s1b[:, t0 + sub * 512:t0 + (sub + 1) * 512],
                                   ps[0:16, sub, :])
            inc(ctx, lm, S("ev"), 1)

    b_ev2 = cnt(ctx, S("ev"))
    b_stN = cnt(ctx, S("st"))
    for mc in range(M8):
        b_mm = cnt(ctx, S("mm"))

        @ctx.block.tensor
        def _(t, mc=mc, b_mm=b_mm):
            wait(t, ctx, S("ev"))
            for tt8 in range(8):
                lm = t.matmul(ps[:, tt8 % 8, :],
                              w2[:, mc * 128:(mc + 1) * 128],
                              s1b[:, tt8 * 512:(tt8 + 1) * 512],
                              start=True, stop=True)
            inc(ctx, lm, S("mm"), 1)

        @ctx.block.vector
        def _(v, mc=mc, b_mm=b_mm):
            v.wait_ge(sm("mm"), b_mm + 1)
            if mc == 0:
                wait(v, ctx, S("ld"))
            if mc >= 1:
                v.wait_ge(sm("st"), b_stN + 32 * mc)
            for tt8 in range(8):
                v.tensor_scalar(out=esf[:, tt8 * 512:(tt8 + 1) * 512],
                                in0=ps[:, tt8 % 8, :],
                                scalar1=bs2t[:, mc:mc + 1], scalar2=32.0,
                                op0=ALU.add, op1=ALU.min)
            lm = v.tensor_scalar_max(esf[:, :], esf[:, :], -32.0)
            inc(ctx, lm, S("ev"), 1)

        @ctx.block.scalar
        def _(sce, mc=mc):
            wait(sce, ctx, S("ev"))
            ii = sce.activation(out=csf[:, :], in_=esf[:, :], func=AF.Exp)
            inc(ctx, ii, S("sc"), 1)

        @ctx.block.vector
        def _(v, mc=mc):
            wait(v, ctx, S("sc"))
            v.tensor_copy(eb[:, :], csf[:, :])
            v.tensor_tensor_scan(esf[:, :], csf[:, :], csf[:, :], 0.0,
                                 ALU.add, ALU.bypass)
            lm = v.reciprocal(csf[:, :], esf[:, :])
            inc(ctx, lm, S("ev"), 1)

        @ctx.block.sync
        def _(s, mc=mc):
            wait(s, ctx, S("ev"))
            ii = s.dma_start(out=es_d[mc * 128:(mc + 1) * 128, :], in_=eb[:, :])
            inc(ctx, ii, S("st"), 16)
            ii = s.dma_start(out=enz_d[mc * 128:(mc + 1) * 128, :], in_=csf[:, :])
            inc(ctx, ii, S("st"), 16)

    b_mm3 = cnt(ctx, S("mm"))
    b_ev3 = cnt(ctx, S("ev"))
    b_st3 = cnt(ctx, S("st"))

    @ctx.block.tensor
    def _(t, b_mm3=b_mm3):
        wait(t, ctx, S("sc"))
        for tc in range(32):
            for ng in range(2):
                gi = tc * 2 + ng
                if gi >= 2:
                    t.wait_ge(sm("ev"), b_ev3 + gi - 1)
                lm = t.matmul(ps[:, gi % 2, :],
                              s1b[:, tc * 128:(tc + 1) * 128],
                              w2[:, ng * 512:(ng + 1) * 512],
                              start=True, stop=True)
                inc(ctx, lm, S("mm"), 1)

    @ctx.block.vector
    def _(v, b_mm3=b_mm3, b_st3=b_st3):
        for tc in range(32):
            for ng in range(2):
                gi = tc * 2 + ng
                v.wait_ge(sm("mm"), b_mm3 + gi + 1)
                if gi >= 2:
                    v.wait_ge(sm("st"), b_st3 + 16 * gi)
                hb = (gi % 2) * 1024
                v.tensor_tensor(etm[:, hb + 0:hb + 512], ps[:, gi % 2, :],
                                bs2bc[:, ng * 512:(ng + 1) * 512], ALU.add)
                v.tensor_scalar_min(etm[:, hb:hb + 512], etm[:, hb:hb + 512], 32.0)
                lm = v.tensor_scalar_max(etm[:, hb:hb + 512], etm[:, hb:hb + 512],
                                         -32.0)
                inc(ctx, lm, S("ev"), 1)

    @ctx.block.scalar
    def _(sce, b_ev3=b_ev3, b_st3=b_st3):
        for tc in range(32):
            for ng in range(2):
                gi = tc * 2 + ng
                sce.wait_ge(sm("ev"), b_ev3 + gi + 1)
                hb = (gi % 2) * 1024
                ii = sce.activation(out=etb[:, hb:hb + 512], in_=etm[:, hb:hb + 512],
                                    func=AF.Exp)
                inc(ctx, ii, S("sc"), 1)

    @ctx.block.sync
    def _(s, b_st3=b_st3):
        sc_b = cnt(ctx, S("sc")) - 64
        for tc in range(32):
            for ng in range(2):
                gi = tc * 2 + ng
                s.wait_ge(sm("sc"), sc_b + gi + 1)
                hb = (gi % 2) * 1024
                ii = s.dma_start(
                    out=estm_d[tc * 128:(tc + 1) * 128, ng * 512:(ng + 1) * 512],
                    in_=etb[:, hb:hb + 512])
                inc(ctx, ii, S("st"), 16)

    fence_dma(ctx, "sync", S("st"))
    barrier(ctx)
    end_scope(ctx)


def phase_attn(ctx, q_d, g_d, k_d, vtm_d, es_d, enzm_d, estm_d, masksT, gnw_in,
               idf_in, og_d, tag="O"):
    begin_scope(ctx)
    S = lambda n: n
    sm = lambda n: ctx.sems[n]
    SCALE = float(128.0 ** -0.5)
    C = ctx.cnt

    k4 = sbuf(ctx, tag + "k4", [128, 4 * T], BF16)
    v4 = sbuf(ctx, tag + "v4", [128, 4 * T], BF16)
    es4 = sbuf(ctx, tag + "es4", [128, 2 * T], BF16)
    et4 = sbuf(ctx, tag + "et4", [128, 32 * 256], BF16)
    q4 = sbuf(ctx, tag + "q4", [128, 4 * TM], BF16)
    g4 = sbuf(ctx, tag + "g4", [128, 4 * TM], BF16)
    msk = sbuf(ctx, tag + "msk", [128, 16 * 2 * 128], F32)
    gnw = sbuf(ctx, tag + "gnw", [128, 1], F32)
    idf = sbuf(ctx, tag + "idf", [128, 128], F32)
    idb = sbuf(ctx, tag + "idb", [128, 128], BF16)
    enzsl = sbuf(ctx, tag + "ez", [64, 2 * 128], F32)
    at1 = sbuf(ctx, tag + "at1", [128, 2 * 128], BF16)
    at2 = sbuf(ctx, tag + "at2", [128, 2 * 128], BF16)
    ezt = sbuf(ctx, tag + "ezt", [128, 64], F32)
    okz = sbuf(ctx, tag + "okz", [128, 64], F32)
    nm = sbuf(ctx, tag + "nm", [128, 1], F32)
    sm1 = sbuf(ctx, tag + "sm1", [128, 1], F32)
    qv = sbuf(ctx, tag + "qv", [128, 64], BF16)
    qvn = sbuf(ctx, tag + "qvn", [128, 64], BF16)
    qvz = sbuf(ctx, tag + "qvz", [128, 64], BF16)
    qvzf = sbuf(ctx, tag + "qvzf", [128, 128], BF16)
    sq = sbuf(ctx, tag + "sq", [128, 128], BF16)
    v1 = sbuf(ctx, tag + "v1", [128, 128], F32)
    v2 = sbuf(ctx, tag + "v2", [128, 128], F32)
    rstd = sbuf(ctx, tag + "rstd", [128, 128], F32)
    t2 = sbuf(ctx, tag + "t2", [128, 128], F32)
    t3 = sbuf(ctx, tag + "t3", [128, 128], BF16)
    sgl = sbuf(ctx, tag + "sg", [128, 128], BF16)
    og_sb = sbuf(ctx, tag + "og", [128, 4 * 128], BF16)
    pad = sbuf(ctx, tag + "pad", [128, 256], F32)
    pad2 = sbuf(ctx, tag + "pad2", [128, 256], F32)
    ps = psum(ctx, tag + "ps", [128, 5, 512])
    psb = psum(ctx, tag + "psb", [128, 1, 1024], BF16)
    ones = ctx.ones

    @ctx.block.sync
    def _(s):
        for dst, src_ in ((gnw, gnw_in), (idf, idf_in)):
            ii = s.dma_start(out=dst[:, :], in_=src_[:, :])
            inc(ctx, ii, S("ld"), 16)
        ii = s.dma_start(
            out=msk[:, :].rearrange("p (v j q) -> p v j q", j=2, q=128),
            in_=masksT.rearrange("v (j p) q -> p v j q", p=128))
        inc(ctx, ii, S("ld"), 16)

    @ctx.block.gpsimd
    def _(g):
        ii = g.dma_start(out=idb[:, :], in_=idf_in[:, :])
        inc(ctx, ii, S("gld"), 16)

    @ctx.block.vector
    def _(v):
        v.memset(pad2[:, :], 0.0)

    gld_hi = cnt(ctx, S("gld"))

    MS = {}
    sh_ctr = [0]

    for hg in range(4):
        @ctx.block.sync
        def _(s, hg=hg):
            if hg > 0:
                wait(s, ctx, S("ey"))
            for hh in range(4):
                h = hg * 4 + hh
                ii = s.dma_start(out=k4[:, hh * T:(hh + 1) * T],
                                 in_=k_d[h * 128:(h + 1) * 128, :])
                inc(ctx, ii, S("ld"), 16)
                for half in range(2):
                    ii = s.dma_start(
                        out=v4[:, hh * T + half * 2048:hh * T + (half + 1) * 2048].rearrange(
                            "p (j q) -> p j q", q=128),
                        in_=vtm_d[half * 2048:(half + 1) * 2048,
                                  h * 128:(h + 1) * 128].rearrange(
                            "(j p) q -> p j q", p=128))
                    inc(ctx, ii, S("ld"), 16)
                ii = s.dma_start(out=q4[:, hh * TM:(hh + 1) * TM],
                                 in_=q_d[h * 128:(h + 1) * 128, :])
                inc(ctx, ii, S("ld"), 16)
                ii = s.dma_start(out=g4[:, hh * TM:(hh + 1) * TM],
                                 in_=g_d[h * 128:(h + 1) * 128, :])
                inc(ctx, ii, S("ld"), 16)
            for c2 in range(2):
                ii = s.dma_start(out=es4[:, c2 * T:(c2 + 1) * T],
                                 in_=es_d[(hg * 2 + c2) * 128:(hg * 2 + c2 + 1) * 128, :])
                inc(ctx, ii, S("ld"), 16)
            ii = s.dma_start(
                out=et4[:, :].rearrange("p (j q) -> p j q", q=256),
                in_=estm_d.rearrange("(j p) m -> p j m", p=128)[:, :, hg * 256:(hg + 1) * 256])
            inc(ctx, ii, S("ld"), 16)

        ld_hi = cnt(ctx, S("ld"))

        for v_ in range(16):
            J = 2 * v_ + 2
            for hh in range(4):
                h = hg * 4 + hh
                sh = sh_ctr[0]
                sh_ctr[0] += 1
                ebuf = sh % 2
                hb64 = (hh % 2) * 64

                def evname(jc, J=J):
                    if jc >= J - 2:
                        return "eg"
                    return "sa" if jc % 2 == 0 else "eg"

                sa_c = C["sa"]
                eg_c = C["eg"]
                ev1 = []
                for jc in range(J):
                    if evname(jc) == "sa":
                        sa_c += 1
                        ev1.append(("sa", sa_c))
                    else:
                        eg_c += 1
                        ev1.append(("eg", eg_c))
                ev2 = []
                for jc in range(J):
                    if evname(jc) == "sa":
                        sa_c += 1
                        ev2.append(("sa", sa_c))
                    else:
                        eg_c += 1
                        ev2.append(("eg", eg_c))
                M = dict(
                    bc0=C["bc"], my0=C["my"], wf0=C["wf"], ew0=C["ew"],
                    enzT=C["ms"] + 1, qvzT=C["ms"] + 2, stat=C["ms"] + 3,
                    ezt=C["ac"] + 1, nmv=C["ac"] + 2, qvz=C["ac"] + 3,
                    qvzf=C["ac"] + 4, v1=C["ac"] + 5, t2=C["ac"] + 6,
                    exp=C["ex"] + 1, sq=C["ex"] + 2, v2=C["ex"] + 3,
                    t3=C["ex"] + 4,
                    ldt=C["lda" if ebuf == 0 else "ldb"] + 16,
                )
                MS[sh] = M

                @ctx.block.sync
                def _(s, v_=v_, h=h, sh=sh, ebuf=ebuf):
                    if sh >= 2:
                        s.wait_ge(sm("ms"), MS[sh - 2]["enzT"])
                    ii = s.dma_start(out=enzsl[:, ebuf * 128:(ebuf + 1) * 128],
                                     in_=enzm_d[h * 64:(h + 1) * 64,
                                                v_ * 128:(v_ + 1) * 128])
                    inc(ctx, ii, S("lda" if ebuf == 0 else "ldb"), 16)

                @ctx.block.tensor
                def _(t, v_=v_, hh=hh, sh=sh, J=J, ebuf=ebuf, hb64=hb64,
                      ev1=ev1, ev2=ev2, ld_hi=ld_hi, M=M):
                    if sh % 64 == 0:
                        t.wait_ge(sm("ld"), ld_hi)
                        t.wait_ge(sm("gld"), gld_hi)
                    if sh == 1 and "okz0" in ctx.dbg:
                        wait(t, ctx, S("st"))
                    for jc in range(J):
                        if jc >= 2:
                            t.wait_ge(sm(ev1[jc - 2][0]), ev1[jc - 2][1])
                        lm = t.matmul(ps[:, jc % 2, 0:128],
                                      k4[:, hh * T + jc * 128:hh * T + (jc + 1) * 128],
                                      q4[:, hh * TM + v_ * 128:hh * TM + (v_ + 1) * 128],
                                      start=True, stop=True)
                        lm.then_inc(sm("bc"), 1)
                        C["bc"] += 1
                        if jc >= 1:
                            jp = jc - 1
                            t.wait_ge(sm(ev1[jp][0]), ev1[jp][1])
                            if jp == 0 and sh >= 1:
                                t.wait_ge(sm("ac"), MS[sh - 1]["nmv"])
                            lm = t.matmul(ps[:, 2, 0:64],
                                          at1[:, (jp % 2) * 128:(jp % 2 + 1) * 128],
                                          et4[:, jp * 256 + hh * 64:jp * 256 + (hh + 1) * 64],
                                          start=(jp == 0), stop=False)
                            lm.then_inc(sm("my"), 1)
                            C["my"] += 1
                    jp = J - 1
                    t.wait_ge(sm(ev1[jp][0]), ev1[jp][1])
                    lm = t.matmul(ps[:, 2, 0:64],
                                  at1[:, (jp % 2) * 128:(jp % 2 + 1) * 128],
                                  et4[:, jp * 256 + hh * 64:jp * 256 + (hh + 1) * 64],
                                  start=False, stop=True)
                    lm.then_inc(sm("my"), 1)
                    C["my"] += 1
                    t.wait_ge(sm("lda" if ebuf == 0 else "ldb"), M["ldt"])
                    if sh >= 1:
                        t.wait_ge(sm("ac"), MS[sh - 1]["v1"])
                        t.wait_ge(sm("ac"), MS[sh - 1]["qvz"])
                    lm = t.transpose(ps[0:128, 4, 0:64],
                                     enzsl[:, ebuf * 128:(ebuf + 1) * 128],
                                     idf[0:64, 0:64])
                    lm.then_inc(sm("ms"), 1)
                    C["ms"] += 1
                    t.wait_ge(sm("ac"), M["qvz"])
                    if sh >= 1:
                        t.wait_ge(sm("ac"), MS[sh - 1]["qvzf"])
                    lm = t.transpose(psb[hb64:hb64 + 64, 0, 0:128], qvz[:, :],
                                     idb[:, :])
                    lm.then_inc(sm("ms"), 1)
                    C["ms"] += 1
                    for jc in range(J):
                        if jc == 0:
                            t.wait_ge(sm("ac"), M["qvzf"])
                        if jc >= 2:
                            t.wait_ge(sm(ev2[jc - 2][0]), ev2[jc - 2][1])
                        lm = t.matmul(ps[:, jc % 2, 0:128],
                                      es4[hb64:hb64 + 64,
                                          (hh // 2) * T + jc * 128:(hh // 2) * T + (jc + 1) * 128],
                                      qvzf[hb64:hb64 + 64, :],
                                      start=True, stop=True)
                        lm.then_inc(sm("wf"), 1)
                        C["wf"] += 1
                        if jc >= 1:
                            jp = jc - 1
                            t.wait_ge(sm(ev2[jp][0]), ev2[jp][1])
                            if jp == 0 and sh >= 1:
                                t.wait_ge(sm("ac"), MS[sh - 1]["t2"])
                            lm = t.matmul(ps[:, 3, 0:128],
                                          v4[:, hh * T + jp * 128:hh * T + (jp + 1) * 128],
                                          at2[:, (jp % 2) * 128:(jp % 2 + 1) * 128],
                                          start=(jp == 0), stop=False)
                            lm.then_inc(sm("ew"), 1)
                            C["ew"] += 1
                    jp = J - 1
                    t.wait_ge(sm(ev2[jp][0]), ev2[jp][1])
                    lm = t.matmul(ps[:, 3, 0:128],
                                  v4[:, hh * T + jp * 128:hh * T + (jp + 1) * 128],
                                  at2[:, (jp % 2) * 128:(jp % 2 + 1) * 128],
                                  start=(jp == 0), stop=True)
                    lm.then_inc(sm("ew"), 1)
                    C["ew"] += 1
                    t.wait_ge(sm("ex"), M["sq"])
                    t.wait_ge(sm("ac"), M["ezt"])
                    lm = t.matmul(ps[:, 4, 0:128], ones[:, :], sq[:, :],
                                  start=True, stop=True)
                    lm.then_inc(sm("ms"), 1)
                    C["ms"] += 1

                @ctx.block.scalar
                def _(sce, v_=v_, hh=hh, sh=sh, J=J, hb64=hb64, M=M):
                    for jc in range(J):
                        if evname(jc, J) != "sa":
                            continue
                        sce.wait_ge(sm("bc"), M["bc0"] + jc + 1)
                        if jc >= 2:
                            sce.wait_ge(sm("my"), M["my0"] + jc - 1)
                        ii = sce.activation(out=at1[:, (jc % 2) * 128:(jc % 2 + 1) * 128],
                                            in_=ps[:, jc % 2, 0:128], func=AF.Copy,
                                            scale=SCALE)
                        ii.then_inc(sm("sa"), 1)
                        C["sa"] += 1
                    sce.wait_ge(sm("ac"), M["nmv"])
                    ii = sce.activation(out=qv[:, :], in_=okz[:, :], func=AF.Exp,
                                        bias=nm[:, 0:1])
                    ii.then_inc(sm("ex"), 1)
                    C["ex"] += 1
                    for jc in range(J):
                        if evname(jc, J) != "sa":
                            continue
                        sce.wait_ge(sm("wf"), M["wf0"] + jc + 1)
                        if jc >= 2:
                            sce.wait_ge(sm("ew"), M["ew0"] + jc - 1)
                        ii = sce.activation(out=at2[:, (jc % 2) * 128:(jc % 2 + 1) * 128],
                                            in_=ps[:, jc % 2, 0:128], func=AF.Copy)
                        ii.then_inc(sm("sa"), 1)
                        C["sa"] += 1
                    sce.wait_ge(sm("ew"), M["ew0"] + J)
                    ii = sce.activation(out=sq[:, :], in_=ps[:, 3, 0:128],
                                        func=AF.Square)
                    ii.then_inc(sm("ex"), 1)
                    C["ex"] += 1
                    sce.wait_ge(sm("ac"), M["v1"])
                    ii = sce.activation(out=v2[:, :], in_=v1[:, :], func=AF.Sqrt)
                    ii.then_inc(sm("ex"), 1)
                    C["ex"] += 1
                    sce.wait_ge(sm("ac"), M["t2"])
                    sce.activation(out=t3[:, :], in_=t2[:, :], func=AF.Copy,
                                   scale=gnw[:, 0:1])
                    ii = sce.activation(out=sgl[:, :],
                                        in_=g4[:, hh * TM + v_ * 128:hh * TM + (v_ + 1) * 128],
                                        func=AF.Silu)
                    ii.then_inc(sm("ex"), 1)
                    C["ex"] += 1

                @ctx.block.vector
                def _(v, v_=v_, hh=hh, sh=sh, J=J, ebuf=ebuf, hb64=hb64,
                      M=M):
                    for jc in range(J):
                        if evname(jc, J) != "eg":
                            continue
                        v.wait_ge(sm("bc"), M["bc0"] + jc + 1)
                        if jc >= 2:
                            v.wait_ge(sm("my"), M["my0"] + jc - 1)
                        if jc >= J - 2:
                            lm = v.scalar_tensor_tensor(
                                out=at1[:, (jc % 2) * 128:(jc % 2 + 1) * 128],
                                in0=ps[:, jc % 2, 0:128], scalar=SCALE,
                                in1=msk[:, (v_ * 2 + (jc - (J - 2))) * 128:
                                        (v_ * 2 + (jc - (J - 2)) + 1) * 128],
                                op0=ALU.mult, op1=ALU.mult)
                        else:
                            lm = v.tensor_scalar(
                                out=at1[:, (jc % 2) * 128:(jc % 2 + 1) * 128],
                                in0=ps[:, jc % 2, 0:128], scalar1=SCALE,
                                scalar2=None, op0=ALU.mult)
                        lm.then_inc(sm("eg"), 1)
                        C["eg"] += 1
                    v.wait_ge(sm("ms"), M["enzT"])
                    ii = v.tensor_copy(ezt[:, :], ps[0:128, 4, 0:64])
                    ii.then_inc(sm("ac"), 1)
                    C["ac"] += 1
                    v.wait_ge(sm("my"), M["my0"] + J)
                    v.tensor_copy(pad[:, :], pad2[:, :])
                    v.tensor_tensor(okz[:, :], ps[:, 2, 0:64], ezt[:, :], ALU.mult)
                    v.tensor_copy(pad[:, :], pad2[:, :])
                    ii = v.reduce_max(nm[:, 0:1], okz[:, :],
                                      axis=mybir.AxisListType.X, negate=True)
                    ii.then_inc(sm("ac"), 1)
                    C["ac"] += 1
                    v.wait_ge(sm("ex"), M["exp"])
                    v.reduce_sum(nm[:, 0:1], qv[:, :], axis=mybir.AxisListType.X)
                    v.tensor_copy(pad[:, :], pad2[:, :])
                    v.reciprocal(sm1[:, 0:1], nm[:, 0:1])
                    v.tensor_copy(pad[:, :], pad2[:, :])
                    v.tensor_scalar(out=qvn[:, :], in0=qv[:, :],
                                    scalar1=sm1[:, 0:1], scalar2=None,
                                    op0=ALU.mult)
                    v.tensor_copy(pad[:, :], pad2[:, :])
                    ii = v.tensor_tensor(qvz[:, :], qvn[:, :], ezt[:, :], ALU.mult)
                    ii.then_inc(sm("ac"), 1)
                    C["ac"] += 1
                    v.wait_ge(sm("ms"), M["qvzT"])
                    if sh >= 2:
                        v.wait_ge(sm("wf"), MS[sh - 2]["wf0"] + 2 * (v_ - 0) if False
                                  else MS[sh - 2]["wf0"] + (2 * ((sh - 2) % 64 // 4) + 2))
                    ii = v.tensor_copy(qvzf[hb64:hb64 + 64, :],
                                       psb[hb64:hb64 + 64, 0, 0:128])
                    ii.then_inc(sm("ac"), 1)
                    C["ac"] += 1
                    for jc in range(J):
                        if evname(jc, J) != "eg":
                            continue
                        v.wait_ge(sm("wf"), M["wf0"] + jc + 1)
                        if jc >= 2:
                            v.wait_ge(sm("ew"), M["ew0"] + jc - 1)
                        if jc >= J - 2:
                            lm = v.scalar_tensor_tensor(
                                out=at2[:, (jc % 2) * 128:(jc % 2 + 1) * 128],
                                in0=ps[:, jc % 2, 0:128], scalar=1.0,
                                in1=msk[:, (v_ * 2 + (jc - (J - 2))) * 128:
                                        (v_ * 2 + (jc - (J - 2)) + 1) * 128],
                                op0=ALU.mult, op1=ALU.mult)
                        else:
                            lm = v.tensor_copy(
                                at2[:, (jc % 2) * 128:(jc % 2 + 1) * 128],
                                ps[:, jc % 2, 0:128])
                        lm.then_inc(sm("eg"), 1)
                        C["eg"] += 1
                    v.wait_ge(sm("ms"), M["stat"])
                    ii = v.tensor_scalar(out=v1[:, :], in0=ps[:, 4, 0:128],
                                         scalar1=1.0 / 128.0, scalar2=EPS,
                                         op0=ALU.mult, op1=ALU.add)
                    ii.then_inc(sm("ac"), 1)
                    C["ac"] += 1
                    v.wait_ge(sm("ex"), M["v2"])
                    v.reciprocal(rstd[:, :], v2[:, :])
                    v.tensor_copy(pad[:, :], pad2[:, :])
                    ii = v.tensor_tensor(t2[:, :], ps[:, 3, 0:128], rstd[:, :],
                                         ALU.mult)
                    ii.then_inc(sm("ac"), 1)
                    C["ac"] += 1
                    v.wait_ge(sm("ex"), M["t3"])
                    if hh == 0:
                        v.wait_ge(sm("st"), C["st"])
                    ii = v.tensor_mul(og_sb[:, hh * 128:(hh + 1) * 128],
                                      t3[:, :], sgl[:, :])
                    ii.then_inc(sm("ey"), 1)
                    C["ey"] += 1

                if "okz0" in ctx.dbg and sh == 0:
                    adm = {}
                    for nm_, tile, wid, dt_ in (
                            ("okz0", okz, 64, F32), ("ezt0", ezt, 64, F32),
                            ("qv0", qv, 64, BF16), ("qvz0", qvz, 64, BF16),
                            ("sq0", sq, 128, BF16), ("t20", t2, 128, F32),
                            ("og0", og_sb, 128, BF16), ("at10", at1, 256, BF16),
                            ("at20", at2, 256, BF16)):
                        adm[nm_] = (ctx.dram_tmp(nm_, [128, wid], dt_), tile, wid)

                    @ctx.block.sync
                    def _(s):
                        wait(s, ctx, S("ey"))
                        for nm_, (d_, tile, wid) in adm.items():
                            ii = s.dma_start(out=d_, in_=tile[:, 0:wid])
                            inc(ctx, ii, S("st"), 16)

                if hh == 3:
                    @ctx.block.sync
                    def _(s, hg=hg, v_=v_):
                        wait(s, ctx, S("ey"))
                        ii = s.dma_start(
                            out=og_d[hg * 512:(hg + 1) * 512,
                                     v_ * 128:(v_ + 1) * 128].rearrange(
                                "(c p) q -> p c q", p=128),
                            in_=og_sb[:, :].rearrange("p (c q) -> p c q", q=128))
                        inc(ctx, ii, S("st"), 16)

    fence_dma(ctx, "sync", S("st"))
    barrier(ctx)
    end_scope(ctx)


def phase_moe(ctx, xn1m_d, w_router, w_gate, w_up, w_down, w_sg, w_su, w_sd,
              id8_in, idf_in, moe_d, tag="Q"):
    begin_scope(ctx)
    S = lambda n: n
    sm = lambda n: ctx.sems[n]
    C = ctx.cnt
    KT = 16
    MT = 16

    a_sb = sbuf(ctx, tag + "a", [128, 2 * TM], BF16)
    xe_sb = sbuf(ctx, tag + "xe", [128, KT * TM], BF16)
    swe = sbuf(ctx, tag + "swe", [128, MT * TM], BF16)
    w_sb = sbuf(ctx, tag + "w", [128, 4 * KT * 128], BF16)
    wr = sbuf(ctx, tag + "wr", [128, 16 * 8], BF16)
    sgt = sbuf(ctx, tag + "sgt", [128, TM], BF16)
    lgf = sbuf(ctx, tag + "lgf", [8, TM], F32)
    lgt = sbuf(ctx, tag + "lgt", [128, 16 * 8], F32)
    mxa = sbuf(ctx, tag + "mxa", [128, 16], F32)
    sca = sbuf(ctx, tag + "sca", [128, 16], F32)
    cet = sbuf(ctx, tag + "cet", [128, 16 * 8], F32)
    cef = sbuf(ctx, tag + "cef", [8, TM], F32)
    rhm = sbuf(ctx, tag + "rhm", [8, TM], F32)
    o8 = sbuf(ctx, tag + "o8", [8, 128], F32)
    id8 = sbuf(ctx, tag + "id8", [8, 8], F32)
    idf = sbuf(ctx, tag + "idf", [128, 128], F32)
    osb = sbuf(ctx, tag + "osb", [128, 2 * TM], F32)
    racc = sbuf(ctx, tag + "racc", [128, 2 * TM], BF16)
    pad = osb
    pad2_sl = slice(256, 512)
    ps = psum(ctx, tag + "ps", [128, 8, 512])

    b_lp = (cnt(ctx, S("lda")), cnt(ctx, S("ldb")))
    b_rmm0 = cnt(ctx, S("mm"))

    @ctx.block.sync
    def _(s):
        ii = s.dma_start(out=id8[:, :], in_=id8_in[:, :])
        inc(ctx, ii, S("ld"), 16)
        ii = s.dma_start(out=idf[:, :], in_=idf_in[:, :])
        inc(ctx, ii, S("ld"), 16)
        for c in range(KT):
            if c >= 2:
                s.wait_ge(sm("mm"), b_rmm0 + c - 1)
            ii = s.dma_start(out=a_sb[:, (c % 2) * TM:(c % 2 + 1) * TM],
                             in_=xn1m_d[c * 128:(c + 1) * 128, :])
            inc(ctx, ii, S("lda" if c % 2 == 0 else "ldb"), 16)

    @ctx.block.gpsimd
    def _(g):
        ii = g.dma_start(out=wr[:, :].rearrange("p (k q) -> p k q", q=8),
                         in_=w_router[0].rearrange("k p q -> p k q"))
        inc(ctx, ii, S("gld"), 16)

    ld_hi = cnt(ctx, S("ld"))
    gld_hi = cnt(ctx, S("gld"))

    @ctx.block.vector
    def _(v):
        v.memset(o8[:, :], 1.0)
        v.memset(pad[:, 256:512], 0.0)
        ii = v.engine_nop()
        inc(ctx, ii, S("ev"), 1)

    @ctx.block.tensor
    def _(t):
        t.wait_ge(sm("ld"), ld_hi)
        t.wait_ge(sm("gld"), gld_hi)
        wait(t, ctx, S("ev"))
        for k in range(KT):
            t.wait_ge(sm("lda" if k % 2 == 0 else "ldb"),
                      b_lp[k % 2] + 16 * (k // 2 + 1))
            for sub in range(4):
                lm = t.matmul(ps[0:8, sub, :], wr[:, k * 8:(k + 1) * 8],
                              a_sb[:, (k % 2) * TM + sub * 512:(k % 2) * TM + (sub + 1) * 512],
                              start=(k == 0), stop=(k == KT - 1))
            lm.then_inc(sm("mm"), 1)
            C["mm"] += 1

    @ctx.block.vector
    def _(v):
        wait(v, ctx, S("mm"))
        for sub in range(4):
            lm = v.tensor_copy(lgf[:, sub * 512:(sub + 1) * 512], ps[0:8, sub, :])
        inc(ctx, lm, S("ev"), 1)

    b_ms = cnt(ctx, S("ms"))
    b_ac = cnt(ctx, S("ac"))

    @ctx.block.tensor
    def _(t):
        wait(t, ctx, S("ev"))
        for tc in range(16):
            if tc >= 2:
                t.wait_ge(sm("ac"), b_ac + tc - 1)
            lm = t.transpose(ps[0:128, 4 + tc % 2, 0:8],
                             lgf[:, tc * 128:(tc + 1) * 128], idf[0:8, 0:8])
            lm.then_inc(sm("ms"), 1)
            C["ms"] += 1

    @ctx.block.vector
    def _(v):
        for tc in range(16):
            v.wait_ge(sm("ms"), b_ms + tc + 1)
            v.tensor_copy(lgt[:, tc * 8:(tc + 1) * 8], ps[0:128, 4 + tc % 2, 0:8])
            v.tensor_copy(pad[:, 0:256], pad[:, 256:512])
            ii = v.reduce_max(mxa[:, tc:tc + 1], lgt[:, tc * 8:(tc + 1) * 8],
                              axis=mybir.AxisListType.X)
            ii.then_inc(sm("ac"), 1)
            C["ac"] += 1

    @ctx.block.scalar
    def _(sce):
        wait(sce, ctx, S("ac"))
        ii = sce.activation(out=sca[:, :], in_=mxa[:, :], func=AF.Sigmoid)
        inc(ctx, ii, S("sc"), 1)

    b_ms2 = cnt(ctx, S("ms"))
    b_ac2 = cnt(ctx, S("ac"))

    @ctx.block.vector
    def _(v):
        wait(v, ctx, S("sc"))
        for tc in range(16):
            v.tensor_scalar(out=cet[:, tc * 8:(tc + 1) * 8],
                            in0=lgt[:, tc * 8:(tc + 1) * 8],
                            scalar1=mxa[:, tc:tc + 1], scalar2=None,
                            op0=ALU.is_equal)
            v.tensor_copy(pad[:, 0:256], pad[:, 256:512])
            ii = v.tensor_scalar(out=cet[:, tc * 8:(tc + 1) * 8],
                                 in0=cet[:, tc * 8:(tc + 1) * 8],
                                 scalar1=sca[:, tc:tc + 1], scalar2=None,
                                 op0=ALU.mult)
            ii.then_inc(sm("ac"), 1)
            C["ac"] += 1

    @ctx.block.tensor
    def _(t):
        for tc in range(16):
            t.wait_ge(sm("ac"), b_ac2 + tc + 1)
            if tc >= 2:
                t.wait_ge(sm("ac"), b_ac2 + 16 + tc - 1)
            lm = t.transpose(ps[0:8, 6 + tc % 2, 0:128],
                             cet[:, tc * 8:(tc + 1) * 8], idf[:, :])
            lm.then_inc(sm("ms"), 1)
            C["ms"] += 1

    @ctx.block.vector
    def _(v):
        for tc in range(16):
            v.wait_ge(sm("ms"), b_ms2 + tc + 1)
            ii = v.tensor_copy(cef[:, tc * 128:(tc + 1) * 128],
                               ps[0:8, 6 + tc % 2, 0:128])
            ii.then_inc(sm("ac"), 1)
            C["ac"] += 1

    if "cefd" in ctx.dbg:
        dmp = {}
        for nm_, tile, shp, dt_ in (("lgtd", lgt, [128, 128], F32),
                                    ("mxad", mxa, [128, 16], F32),
                                    ("scad", sca, [128, 16], F32),
                                    ("cetd", cet, [128, 128], F32),
                                    ("cefd", cef, [8, TM], F32)):
            dmp[nm_] = (ctx.dram_tmp(nm_, shp, dt_), tile)

        @ctx.block.sync
        def _(s):
            wait(s, ctx, S("ac"))
            wait(s, ctx, S("sc"))
            for nm_, (d_, tile) in dmp.items():
                ii = s.dma_start(out=d_, in_=tile[:, :])
                inc(ctx, ii, S("st"), 16)

    def emit_ffn(kind, e):
        wg = w_sg if kind == "shared" else w_gate[e]
        wu = w_su if kind == "shared" else w_up[e]
        wd = w_sd if kind == "shared" else w_down[e]
        xin = xe_sb

        if kind != "shared":
            b_ev = cnt(ctx, S("ev"))
            b_bc = cnt(ctx, S("bc"))

            @ctx.block.vector
            def _(v, e=e):
                wait(v, ctx, S("ac"))
                wait(v, ctx, S("bc"))
                ii = v.tensor_scalar(out=rhm[:, :], in0=cef[:, :],
                                     scalar1=id8[:, e:e + 1], scalar2=None,
                                     op0=ALU.mult)
                inc(ctx, ii, S("ev"), 1)

            @ctx.block.tensor
            def _(t, b_ev=b_ev):
                t.wait_ge(sm("ev"), b_ev + 1)
                wait(t, ctx, S("sc"))
                wait(t, ctx, S("eg"))
                for sub in range(4):
                    lm = t.matmul(ps[:, sub, :], o8[:, :],
                                  rhm[:, sub * 512:(sub + 1) * 512],
                                  start=True, stop=True)
                inc(ctx, lm, S("bc"), 1)

            b_lpx = (cnt(ctx, S("lda")), cnt(ctx, S("ldb")))
            b_acx = cnt(ctx, S("ac"))

            @ctx.block.sync
            def _(s, e=e, b_acx=b_acx):
                wait(s, ctx, S("mm"))
                for c in range(KT):
                    if c >= 2:
                        s.wait_ge(sm("ac"), b_acx + c - 1)
                    ii = s.dma_start(out=a_sb[:, (c % 2) * TM:(c % 2 + 1) * TM],
                                     in_=xn1m_d[c * 128:(c + 1) * 128, :])
                    inc(ctx, ii, S("lda" if c % 2 == 0 else "ldb"), 16)

            @ctx.block.vector
            def _(v, b_bc=b_bc, b_lpx=b_lpx):
                v.wait_ge(sm("bc"), b_bc + 1)
                for c in range(KT):
                    v.wait_ge(sm("lda" if c % 2 == 0 else "ldb"),
                              b_lpx[c % 2] + 16 * (c // 2 + 1))
                    for sub in range(4):
                        lm = v.tensor_tensor(
                            xe_sb[:, c * TM + sub * 512:c * TM + (sub + 1) * 512],
                            a_sb[:, (c % 2) * TM + sub * 512:(c % 2) * TM + (sub + 1) * 512],
                            ps[:, sub, :], ALU.mult)
                    lm.then_inc(sm("ac"), 1)
                    C["ac"] += 1
        else:
            b_acx = cnt(ctx, S("ac"))

            @ctx.block.sync
            def _(s, b_acx=b_acx):
                wait(s, ctx, S("ac"))
                for c in range(KT):
                    ii = s.dma_start(out=xe_sb[:, c * TM:(c + 1) * TM],
                                     in_=xn1m_d[c * 128:(c + 1) * 128, :])
                    inc(ctx, ii, S("ld"), 16)

        if "xe0" in ctx.dbg and kind == "exp" and e == 0:
            xe0_d = ctx.dram_tmp("xe0", [128, KT * TM], BF16)

            @ctx.block.sync
            def _(s):
                wait(s, ctx, S("ac"))
                ii = s.dma_start(out=xe0_d, in_=xe_sb[:, :])
                inc(ctx, ii, S("st"), 16)

        xe_ac = cnt(ctx, S("ac"))
        xe_ld = cnt(ctx, S("ld"))
        b_mm = cnt(ctx, S("mm"))
        b_bc2 = cnt(ctx, S("bc"))
        b_sc = cnt(ctx, S("sc"))
        b_ev2 = cnt(ctx, S("ev"))
        b_wl2 = (cnt(ctx, S("wla")), cnt(ctx, S("wlb")))

        @ctx.block.gpsimd
        def _(g):
            for m in range(MT):
                if m >= 2:
                    g.wait_ge(sm("mm"), b_mm + m - 1)
                buf = (m % 2) * 2
                sn = "wla" if m % 2 == 0 else "wlb"
                for j, wt_ in enumerate((wg, wu)):
                    ii = g.dma_start(
                        out=w_sb[:, (buf + j) * KT * 128:(buf + j + 1) * KT * 128].rearrange(
                            "p (k q) -> p k q", q=128),
                        in_=wt_[m].rearrange("k p q -> p k q"))
                    inc(ctx, ii, S(sn), 16)

        @ctx.block.tensor
        def _(t, xe_ac=xe_ac, xe_ld=xe_ld, b_sc=b_sc, b_ev2=b_ev2):
            if kind != "shared":
                t.wait_ge(sm("ac"), xe_ac)
            else:
                t.wait_ge(sm("ld"), xe_ld)
                wait(t, ctx, S("ac"))
            for m in range(MT):
                t.wait_ge(sm("wla" if m % 2 == 0 else "wlb"),
                          b_wl2[m % 2] + 32 * (m // 2 + 1))
                buf = (m % 2) * 2
                if m >= 1:
                    t.wait_ge(sm("sc"), b_sc + m)
                for k in range(KT):
                    for sub in range(4):
                        lm = t.matmul(
                            ps[:, sub, :],
                            w_sb[:, buf * KT * 128 + k * 128:buf * KT * 128 + (k + 1) * 128],
                            xin[:, k * TM + sub * 512:k * TM + (sub + 1) * 512],
                            start=(k == 0), stop=(k == KT - 1))
                inc(ctx, lm, S("bc"), 1)
                if m >= 1:
                    t.wait_ge(sm("ev"), b_ev2 + m)
                for k in range(KT):
                    for sub in range(4):
                        lm = t.matmul(
                            ps[:, 4 + sub, :],
                            w_sb[:, (buf + 1) * KT * 128 + k * 128:(buf + 1) * KT * 128 + (k + 1) * 128],
                            xin[:, k * TM + sub * 512:k * TM + (sub + 1) * 512],
                            start=(k == 0), stop=(k == KT - 1))
                inc(ctx, lm, S("mm"), 1)

        @ctx.block.scalar
        def _(sce, b_bc2=b_bc2, b_ev2=b_ev2):
            for m in range(MT):
                sce.wait_ge(sm("bc"), b_bc2 + m + 1)
                if m >= 1:
                    sce.wait_ge(sm("ev"), b_ev2 + m)
                for sub in range(4):
                    lm = sce.activation(out=sgt[:, sub * 512:(sub + 1) * 512],
                                        in_=ps[:, sub, :], func=AF.Silu)
                lm.then_inc(sm("sc"), 1)
                C["sc"] += 1

        @ctx.block.vector
        def _(v, b_mm=b_mm, b_sc=b_sc):
            for m in range(MT):
                v.wait_ge(sm("sc"), b_sc + m + 1)
                v.wait_ge(sm("mm"), b_mm + m + 1)
                for sub in range(4):
                    lm = v.tensor_tensor(swe[:, m * TM + sub * 512:m * TM + (sub + 1) * 512],
                                         sgt[:, sub * 512:(sub + 1) * 512],
                                         ps[:, 4 + sub, :], ALU.mult)
                inc(ctx, lm, S("ev"), 1)

        if "swe0" in ctx.dbg and kind == "exp" and e == 0:
            swe0_d = ctx.dram_tmp("swe0", [128, MT * TM], BF16)

            @ctx.block.sync
            def _(s):
                wait(s, ctx, S("ev"))
                ii = s.dma_start(out=swe0_d, in_=swe[:, :])
                inc(ctx, ii, S("st"), 16)

        b_my = cnt(ctx, S("my"))
        b_eg = cnt(ctx, S("eg"))
        b_gst = cnt(ctx, S("gst"))
        b_wl3 = (cnt(ctx, S("aca")), cnt(ctx, S("acb")))

        b_lda2 = (cnt(ctx, S("gla")), cnt(ctx, S("glb")))

        @ctx.block.gpsimd
        def _(g, b_my=b_my, b_eg=b_eg):
            wait(g, ctx, S("gst"))
            wait(g, ctx, S("mm"))
            wait(g, ctx, S("bc"))
            for m in range(MT):
                if kind != "shared":
                    if m >= 2:
                        g.wait_ge(sm("eg"), b_eg + m - 1)
                    ii = g.dma_start(out=racc[:, (m % 2) * TM:(m % 2 + 1) * TM],
                                     in_=moe_d[m * 128:(m + 1) * 128, :])
                    inc(ctx, ii, S("gla" if m % 2 == 0 else "glb"), 16)
                if m >= 2:
                    g.wait_ge(sm("my"), b_my + m - 1)
                sn = "aca" if m % 2 == 0 else "acb"
                buf = (m % 2) * 2
                ii = g.dma_start(
                    out=w_sb[:, buf * KT * 128:(buf + 1) * KT * 128].rearrange(
                        "p (k q) -> p k q", q=128),
                    in_=wd[m].rearrange("k p q -> p k q"))
                inc(ctx, ii, S(sn), 16)
                if m >= 1:
                    g.wait_ge(sm("eg"), b_eg + m)
                    ob = ((m - 1) % 2) * TM
                    ii = g.dma_start(out=moe_d[(m - 1) * 128:m * 128, :],
                                     in_=osb[:, ob:ob + TM])
                    inc(ctx, ii, S("gst"), 16)
            g.wait_ge(sm("eg"), b_eg + MT)
            ob = ((MT - 1) % 2) * TM
            ii = g.dma_start(out=moe_d[(MT - 1) * 128:MT * 128, :],
                             in_=osb[:, ob:ob + TM])
            inc(ctx, ii, S("gst"), 16)

        @ctx.block.tensor
        def _(t, b_eg=b_eg):
            wait(t, ctx, S("ev"))
            for m in range(MT):
                t.wait_ge(sm("aca" if m % 2 == 0 else "acb"),
                          b_wl3[m % 2] + 16 * (m // 2 + 1))
                bs = (m % 2) * 4
                if m >= 2:
                    t.wait_ge(sm("eg"), b_eg + m - 1)
                for k in range(KT):
                    for sub in range(4):
                        lm = t.matmul(
                            ps[:, bs + sub, :],
                            w_sb[:, ((m % 2) * 2) * KT * 128 + k * 128:((m % 2) * 2) * KT * 128 + (k + 1) * 128],
                            swe[:, k * TM + sub * 512:k * TM + (sub + 1) * 512],
                            start=(k == 0), stop=(k == KT - 1))
                inc(ctx, lm, S("my"), 1)

        @ctx.block.vector
        def _(v, b_my=b_my, b_gst=b_gst, b_lda2=b_lda2):
            for m in range(MT):
                v.wait_ge(sm("my"), b_my + m + 1)
                if m >= 2:
                    v.wait_ge(sm("gst"), b_gst + 16 * m)
                if kind != "shared":
                    v.wait_ge(sm("gla" if m % 2 == 0 else "glb"),
                              b_lda2[m % 2] + 16 * (m // 2 + 1))
                bs = (m % 2) * 4
                ob = (m % 2) * TM
                for sub in range(4):
                    sl5 = slice(ob + sub * 512, ob + (sub + 1) * 512)
                    if kind == "shared":
                        lm = v.tensor_copy(osb[:, sl5], ps[:, bs + sub, :])
                    else:
                        lm = v.tensor_add(osb[:, sl5], ps[:, bs + sub, :],
                                          racc[:, sl5])
                inc(ctx, lm, S("eg"), 1)

    emit_ffn("shared", -1)
    for e in range(8):
        emit_ffn("exp", e)

    fence_dma(ctx, "gpsimd", S("gst"))
    barrier(ctx)
    end_scope(ctx)


def tile_wT(w):
    dout, din = w.shape
    MT, KT = dout // 128, din // 128
    return np.ascontiguousarray(w.reshape(MT, 128, KT, 128).transpose(0, 2, 3, 1))


def tile_w(w):
    din, dout = w.shape
    KT, MT = din // 128, dout // 128
    return np.ascontiguousarray(w.reshape(KT, 128, MT, 128).transpose(2, 0, 1, 3))


def colmajor(wvec):
    return np.ascontiguousarray(wvec.reshape(-1, 128).T)


def rope_tables():
    pos = np.arange(T, dtype=np.float64)
    inv = 1.0 / (10000.0 ** (np.arange(0, 128, 2, dtype=np.float64) / 128.0))
    ang = pos[None, :] * inv[:, None]
    return (np.cos(ang).astype(np.float32), np.sin(ang).astype(np.float32))


def build_masksT(side):
    tri = np.triu(np.ones((128, 128), np.float32))
    m = np.zeros((16, 256, 128), np.float32)
    for v in range(16):
        if side == 0:
            m[v, 0:128] = tri
            m[v, 128:256] = 0.0
        else:
            m[v, 0:128] = 1.0
            m[v, 128:256] = tri
    return m


def my_token_cols(side):
    return np.concatenate(
        [np.arange((2 * p + side) * 128, (2 * p + side) * 128 + 128)
         for p in range(16)])


def tile_rope_w(w):
    wT = np.ascontiguousarray(w.T)
    return np.ascontiguousarray(
        wT.reshape(16, 128, 16, 2, 64).transpose(2, 3, 0, 1, 4))


def stage_core_inputs(inputs, b, side):
    x = inputs["L0_attn_x"][b]
    w_in_p = np.zeros((PROJ_PAD, D), np.float32)
    w_in_p[:4256] = inputs["L0_attn_in_proj_w"]
    m = {
        "xT": np.ascontiguousarray(x.T),
        "ln1_w": colmajor(inputs["L0_attn_ln1_w"]),
        "w_in": tile_wT(w_in_p),
        "w_ffn": tile_w(inputs["L0_ffn_weight"]),
        "ffn_scale": colmajor(inputs["L0_ffn_scale"]),
        "ones_f": np.ones((128, 128), np.float32),
        "conv_w": np.ascontiguousarray(
            inputs["L0_attn_conv_w"].reshape(XBC, 4)),
        "conv_b": np.ascontiguousarray(
            inputs["L0_attn_conv_b"].reshape(17, 128).T),
        "a_log": inputs["L0_attn_A_log"].reshape(NH, 1),
        "dt_bias": inputs["L0_attn_dt_bias"].reshape(NH, 1),
        "d_rep": colmajor(np.repeat(inputs["L0_attn_D_param"], 64)),
        "mnorm_w": colmajor(inputs["L0_attn_mnorm_w"]),
        "ln2_w": colmajor(inputs["L0_attn_ln2_w"]),
        "w_out": tile_wT(inputs["L0_attn_out_proj_w"]),
        "w_mlp_in": tile_wT(inputs["L0_attn_mlp_in_w"]),
        "w_mlp_out": tile_wT(inputs["L0_attn_mlp_out_w"]),
        "ident32": np.eye(NH, dtype=np.float32),
        "scan_mask": np.triu(np.ones((128, 128), np.float32)),
        "ident_f": np.eye(128, dtype=np.float32),
        "ident8": np.eye(8, dtype=np.float32),
        "side": np.array([[side]], np.int32),
    }
    cosf, sinf = rope_tables()
    myc = my_token_cols(side)
    m.update({
        "wq_t": tile_rope_w(inputs["L1_attn_Wq"]),
        "wk_t": tile_rope_w(inputs["L1_attn_Wk"]),
        "wv_r": np.ascontiguousarray(inputs["L1_attn_Wv"].T),
        "wg_t": tile_wT(inputs["L1_attn_Wg"]),
        "wo_t": tile_wT(inputs["L1_attn_Wo"]),
        "ws1T": np.ascontiguousarray(inputs["L1_attn_Ws1"].T),
        "ws2T": np.ascontiguousarray(inputs["L1_attn_Ws2"].T),
        "bs2c": np.ascontiguousarray(inputs["L1_attn_bs2"].reshape(8, 128).T),
        "bs2b": np.ascontiguousarray(
            np.tile(inputs["L1_attn_bs2"], (128, 1))),
        "gnw": inputs["L1_attn_g_norm_weight"].reshape(128, 1),
        "cos_k": cosf,
        "sin_k": sinf,
        "cos_q": np.ascontiguousarray(cosf[:, myc]),
        "sin_q": np.ascontiguousarray(sinf[:, myc]),
        "masksT": build_masksT(side),
        "w_router": np.ascontiguousarray(
            inputs["L1_ffn_router_w"].reshape(16, 128, 1, 8).transpose(2, 0, 1, 3)),
        "w_gate": np.stack([tile_w(inputs["L1_ffn_gate_w"][e]) for e in range(8)]),
        "w_up": np.stack([tile_w(inputs["L1_ffn_up_w"][e]) for e in range(8)]),
        "w_down": np.stack([tile_w(inputs["L1_ffn_down_w"][e]) for e in range(8)]),
        "w_sg": tile_w(inputs["L1_ffn_shared_gate_w"]),
        "w_su": tile_w(inputs["L1_ffn_shared_up_w"]),
        "w_sd": tile_w(inputs["L1_ffn_shared_down_w"]),
    })
    return m


def declared_input_names(nc):
    names = set()
    for alloc in nc.m.functions[0].allocations:
        if isinstance(alloc, mybir.MemoryLocationSet) and alloc.kind == "ExternalInput":
            names.add(alloc.memorylocations[0].name)
    names.discard("partition_id")
    return names


def kernel(**inputs):
    nc = build_program()
    names = declared_input_names(nc)
    staged = {}
    in_maps = []
    for c in range(8):
        b, side = c // 2, c % 2
        if (b, side) not in staged or True:
            m = stage_core_inputs(inputs, b, side)
        in_maps.append({k: v for k, v in m.items() if k in names})
    res = run_bass_kernel_spmd(nc, in_maps, core_ids=list(range(8)))
    out = np.empty((4, T, D), np.float32)
    for c in range(8):
        b, side = c // 2, c % 2
        xm = np.asarray(res.results[c]["x2m"], dtype=np.float32)
        blocks = xm.reshape(D, 16, 128)
        out[b].reshape(32, 128, D)[side::2] = blocks.transpose(1, 2, 0)
    return out
